# version 18 (final)
# baseline (speedup 1.0000x reference)
import numpy as np
from collections import Counter
from contextlib import ExitStack

import ml_dtypes
import concourse.bass as bass
import concourse.mybir as mybir
from concourse.bass_utils import run_bass_kernel_spmd

F32 = mybir.dt.float32
BF16 = mybir.dt.bfloat16
AF = mybir.ActivationFunctionType
ALU = mybir.AluOpType
AX = mybir.AxisListType
BFD = ml_dtypes.bfloat16

NCORE = 8
L, DM, DI, DS, DR, HF = 8192, 2048, 4096, 16, 128, 5632
DSL, HSL, TSL = DI // NCORE, HF // NCORE, L // NCORE
TA, NCA = 512, 16
TB, NCB = 512, 16
TC, NCC = 512, 2
NKH = HF // 128
NKY = DI // 128
MAGIC = 12582912.0
QCOUNT = float(HF * DM)
EPS_LN = 1e-5
W1T = [128, 128, 128, 128, 128, 64]
RG = [list(range(NCORE))]

DMA_SEMS = (["qp", "qw0", "qw1", "qc1", "qc2", "qc3",
             "qx0", "qx1", "qx2", "qx3",
             "qz0", "qz1", "qz2", "qz3",
             "qxc0", "qxc1", "qxc2", "qxc3",
             "qxr", "qbc", "qbx", "qdt0", "qdt1",
             "qgh", "qgs0", "qgs1", "qgy", "qgw0", "qgw1",
             "qcx0", "qcx1", "qo", "qbo"]
            + [f"cxs{i}" for i in range(16)]
            + [f"chs{i}" for i in range(8)]
            + [f"cys{i}" for i in range(8)])
SEM_NAMES = (["st", "sv", "ss", "sp", "cg", "cw3", "cwo"]
             + [f"cx{i}" for i in range(4)]
             + [f"ch{i}" for i in range(NCORE)]
             + [f"cy{i}" for i in range(NCORE)]
             + DMA_SEMS)


def _build():
    nc = bass.Bass("TRN2", target_bir_lowering=False)

    d_x = nc.dram_tensor("x_full", [L, DM], F32, kind="ExternalInput").ap()
    d_xmy = nc.dram_tensor("x_myslice", [TSL, DM], F32, kind="ExternalInput").ap()
    d_win = nc.dram_tensor("w_in_sl", [2 * DSL, DM], F32, kind="ExternalInput").ap()
    d_cw = nc.dram_tensor("conv_w_t", [128, 16], F32, kind="ExternalInput").ap()
    d_cb = nc.dram_tensor("conv_b_t", [128, 4], F32, kind="ExternalInput").ap()
    d_wx = nc.dram_tensor("w_x_sl", [DR + 2 * DS, DSL], F32, kind="ExternalInput").ap()
    d_wdt = nc.dram_tensor("w_dt_sl", [DSL, DR], F32, kind="ExternalInput").ap()
    d_bdt = nc.dram_tensor("b_dt_t", [128, 4], F32, kind="ExternalInput").ap()
    d_alog = nc.dram_tensor("a_log_t", [128, 64], F32, kind="ExternalInput").ap()
    d_dd = nc.dram_tensor("d_t", [128, 4], F32, kind="ExternalInput").ap()
    d_wo = nc.dram_tensor("w_out_sl", [DM, DSL], F32, kind="ExternalInput").ap()
    d_bo = nc.dram_tensor("b_out", [DM], F32, kind="ExternalInput").ap()
    d_w1 = nc.dram_tensor("w1_sl", [HSL, DM], F32, kind="ExternalInput").ap()
    d_w2 = nc.dram_tensor("w2_sl", [HSL, DM], F32, kind="ExternalInput").ap()
    d_w3 = nc.dram_tensor("w3_sl", [DM, HSL], F32, kind="ExternalInput").ap()
    d_id = nc.dram_tensor("ident", [128, 128], BF16, kind="ExternalInput").ap()
    d_const = nc.dram_tensor("consts", [128, 4], F32, kind="ExternalInput").ap()
    d_out = nc.dram_tensor("out_sl", [TSL, DM], F32, kind="ExternalOutput").ap()

    d_xdp = [nc.dram_tensor(f"xdp{g}", [DR + 2 * DS, 4 * TA], F32) for g in range(4)]
    d_xdr = [nc.dram_tensor(f"xdr{g}", [DR + 2 * DS, 4 * TA], F32) for g in range(4)]
    d_bc16 = nc.dram_tensor("bc16", [2 * DS, L], BF16)
    d_dtb = nc.dram_tensor("dtb", [128, L], BF16)
    d_xc = nc.dram_tensor("xc_f", [DSL, L], F32)
    d_sz = nc.dram_tensor("sz_f", [DSL, L], F32)
    d_hin = [nc.dram_tensor(f"hin{b}", [HSL, TSL], BF16) for b in range(NCORE)]
    d_hcat = nc.dram_tensor("hcat", [NCORE * HF, TSL], BF16)
    d_yin = [nc.dram_tensor(f"yin{b}", [DSL, TSL], BF16) for b in range(NCORE)]
    d_ycat = nc.dram_tensor("ycat", [NCORE * DI, TSL], BF16)
    d_w3qt_sl = nc.dram_tensor("w3qt_sl", [HSL, DM], BF16)
    d_w3qt = nc.dram_tensor("w3qt", [HF, DM], BF16)
    d_wot_sl = nc.dram_tensor("wot_sl", [DSL, DM], BF16)
    d_wot = nc.dram_tensor("wot", [DI, DM], BF16)
    d_gamp = nc.dram_tensor("gamp", [3, 1], F32)
    d_gamr = nc.dram_tensor("gamr", [3, 1], F32)
    d_inv = nc.dram_tensor("invq", [3, 1], F32)

    C = Counter()

    with ExitStack() as st0:
        S = {nm: st0.enter_context(nc.semaphore(nm)) for nm in SEM_NAMES}
        blk = st0.enter_context(nc.Block())

        def dma(q, qn, dst, src):
            q.dma_start(out=dst, in_=src).then_inc(S[qn], 16)
            C[qn] += 16
            return C[qn]

        def ms(instr, sn):
            instr.then_inc(S[sn], 1)
            C[sn] += 1
            return C[sn]

        def emit_barrier():
            bc = {k: C[k] for k in (["st", "sv", "ss", "sp"] + DMA_SEMS)}
            for nm in ("sync", "vector", "scalar", "tensor", "gpsimd"):
                @getattr(blk, nm)
                def _(e, bc=bc):
                    for k, v in bc.items():
                        if v:
                            e.wait_ge(S[k], v)

        sb0 = lambda nm, sh, dt=F32: st0.enter_context(nc.sbuf_tensor(nm, sh, dt))
        s_cw = sb0("s_cw", [128, 16])
        s_cb = sb0("s_cb", [128, 4])
        s_bdt = sb0("s_bdt", [128, 4])
        s_aneg = sb0("s_aneg", [128, 64])
        s_dd = sb0("s_dd", [128, 4])
        s_id = sb0("s_id", [128, 128], BF16)
        s_wdtT = sb0("s_wdtT", [128, 4 * 128], BF16)
        s_invb = sb0("s_invb", [128, 3])
        s_const = sb0("s_const", [128, 4])
        nc.const_aps.register(F32, MAGIC, s_const[:, 0:1])
        nc.const_aps.register(F32, 1.0, s_const[:, 1:2])
        nc.const_aps.register(F32, EPS_LN, s_const[:, 2:3])
        nc.const_aps.register(F32, 0.0, s_const[:, 3:4])

        pA = st0.enter_context(nc.psum_tensor("pA", [128, 512], F32))
        pB = st0.enter_context(nc.psum_tensor("pB", [128, 512], F32))
        pC2 = st0.enter_context(nc.psum_tensor("pC2", [128, 512], F32))
        pD = st0.enter_context(nc.psum_tensor("pD", [128, 512], F32))
        pW32 = st0.enter_context(nc.psum_tensor("pW32", [32, 512], F32))
        pT0 = st0.enter_context(nc.psum_tensor("pT0", [128, 512], BF16))
        pT1 = st0.enter_context(nc.psum_tensor("pT1", [128, 512], BF16))
        pG = st0.enter_context(nc.psum_tensor("pG", [3, 1], F32))
        free_after = {}

        def t_guard(t, p):
            k = p.name if hasattr(p, 'name') else str(p)
            if k in free_after:
                sn, v = free_after[k]
                t.wait_ge(S[sn], v)

        def set_free(p, sn, v):
            k = p.name if hasattr(p, 'name') else str(p)
            free_after[k] = (sn, v)

        tp_state = {"par": 0, "last": {0: 0, 1: 0}}

        def pe_transpose(src_ap, rows, cols, dst_ap, wait_sv=0, wait_ss=0):
            par = tp_state["par"] % 2
            tp_state["par"] += 1
            pT = pT0 if par == 0 else pT1

            @blk.tensor
            def _(t, pT=pT, src_ap=src_ap, rows=rows, cols=cols, par=par,
                  wait_sv=wait_sv, wait_ss=wait_ss):
                if wait_sv:
                    t.wait_ge(S["sv"], wait_sv)
                if wait_ss:
                    t.wait_ge(S["ss"], wait_ss)
                if tp_state["last"][par]:
                    t.wait_ge(S["sv"], tp_state["last"][par])
                i = t.transpose(pT[0:cols, 0:rows], src_ap, s_id[0:rows, 0:rows])
                ms(i, "st")
            tdone = C["st"]

            @blk.vector
            def _(v, pT=pT, dst_ap=dst_ap, rows=rows, cols=cols, tdone=tdone,
                  par=par):
                v.wait_ge(S["st"], tdone)
                i = v.tensor_copy(dst_ap, pT[0:cols, 0:rows])
                tp_state["last"][par] = ms(i, "sv")
            return C["sv"]

        @blk.sync
        def _(q):
            for dst, src in [(s_cw, d_cw), (s_cb, d_cb), (s_bdt, d_bdt),
                             (s_dd, d_dd), (s_id, d_id), (s_const, d_const)]:
                dma(q, "qp", dst[:], src)
        ld_params = C["qp"]

        with ExitStack() as stQ:
            sbQ = lambda nm, sh, dt=F32: stQ.enter_context(nc.sbuf_tensor(nm, sh, dt))
            s_alog = sbQ("s_alog", [128, 64])
            s_stg = [sbQ(f"s_stg{i}", [128, DM]) for i in range(2)]
            s_scrQ = sbQ("s_scrQ", [128, DM])
            s_abs = sbQ("s_abs", [128, 28])
            s_one = sbQ("s_one", [128, 1])
            s_g3 = sbQ("s_g3", [3, 1])

            @blk.sync
            def _(q):
                dma(q, "qp", s_alog[:], d_alog)
            ld_alog = C["qp"]

            @blk.scalar
            def _(s):
                s.wait_ge(S["qp"], ld_alog)
                i = s.activation(s_aneg[:], s_alog[:], AF.Exp)
                ms(i, "ss")
            aexp = C["ss"]

            @blk.vector
            def _(v):
                v.wait_ge(S["ss"], aexp)
                v.tensor_scalar(s_aneg[:], s_aneg[:], -1.0, None, ALU.mult)
                v.memset(s_abs[:], 0.0)
                i = v.memset(s_one[:], 1.0)
                ms(i, "sv")
            absz = C["sv"]

            stg_gate = [None, None]
            tiles = []
            r0 = 0
            for i, rr in enumerate(W1T):
                tiles.append((d_w1[r0:r0 + rr, :], rr, DM, i))
                r0 += rr
            r0 = 0
            for i, rr in enumerate(W1T):
                tiles.append((d_w2[r0:r0 + rr, :], rr, DM, 6 + i))
                r0 += rr
            for i in range(16):
                tiles.append((d_w3[i * 128:(i + 1) * 128, :], 128, HSL, 12 + i))

            for ti, (src, rr, cc, col) in enumerate(tiles):
                par = ti % 2
                stg = s_stg[par]

                @blk.sync
                def _(q, par=par, stg=stg, src=src, rr=rr, cc=cc):
                    if stg_gate[par]:
                        q.wait_ge(S[stg_gate[par][0]], stg_gate[par][1])
                    dma(q, f"qw{par}", stg[0:rr, 0:cc], src)
                ld = C[f"qw{par}"]

                @blk.scalar
                def _(s, stg=stg, rr=rr, cc=cc, col=col, ld=ld, ti=ti, par=par):
                    s.wait_ge(S[f"qw{par}"], ld)
                    if ti == 0:
                        s.wait_ge(S["sv"], absz)
                    else:
                        s.wait_ge(S["ss"], C["ss"])
                    i = s.activation(s_scrQ[0:rr, 0:cc], stg[0:rr, 0:cc], AF.Abs,
                                     accum_out=s_abs[0:rr, col:col + 1])
                    stg_gate[par] = ("ss", ms(i, "ss"))
            all_abs = C["ss"]

            @blk.vector
            def _(v):
                v.wait_ge(S["ss"], all_abs)
                for k in range(3):
                    cols = list(range(k * 6, k * 6 + 6)) if k < 2 else list(range(12, 28))
                    i = v.tensor_tensor(s_scrQ[:, k:k + 1], s_abs[:, cols[0]:cols[0] + 1],
                                        s_abs[:, cols[1]:cols[1] + 1], ALU.add)
                    for cx in cols[2:]:
                        ms(i, "sv"); v.wait_ge(S["sv"], C["sv"])
                        i = v.tensor_tensor(s_scrQ[:, k:k + 1], s_scrQ[:, k:k + 1],
                                            s_abs[:, cx:cx + 1], ALU.add)
                ms(i, "sv"); v.wait_ge(S["sv"], C["sv"])
                i = v.tensor_copy(s_scrQ[:, 4:7], s_scrQ[:, 0:3])
                ms(i, "sv")
            sums_done = C["sv"]

            @blk.tensor
            def _(t):
                t.wait_ge(S["sv"], sums_done)
                i = t.matmul(pG[:], s_scrQ[:, 4:7], s_one[:], start=True, stop=True)
                ms(i, "st")
            gmm = C["st"]

            @blk.scalar
            def _(s):
                s.wait_ge(S["st"], gmm)
                i = s.copy(s_g3[:], pG[:])
                ms(i, "ss")
            set_free(pG, "ss", C["ss"])
            gev = C["ss"]

            @blk.sync
            def _(q):
                q.wait_ge(S["ss"], gev)
                dma(q, "qc1", d_gamp[:, :], s_g3[:])
            gst = C["qc1"]

            @blk.gpsimd
            def _(g):
                g.wait_ge(S["qc1"], gst)
                g.collective_compute("AllReduce", ALU.add, replica_groups=RG,
                                     ins=[d_gamp.ap().opt()],
                                     outs=[d_gamr.ap().opt()]).then_inc(S["cg"], 1)

            @blk.sync
            def _(q):
                q.wait_ge(S["cg"], 1)
                dma(q, "qc1", s_g3[:], d_gamr[:, :])
            grd = C["qc1"]

            @blk.vector
            def _(v):
                v.wait_ge(S["qc1"], grd)
                i = v.tensor_scalar(s_g3[:], s_g3[:], 1.0 / QCOUNT, 1e-8, ALU.mult, ALU.add)
                ms(i, "sv"); v.wait_ge(S["sv"], C["sv"])
                i = v.reciprocal(s_g3[:], s_g3[:])
                ms(i, "sv")
            invd = C["sv"]

            @blk.sync
            def _(q):
                q.wait_ge(S["sv"], invd)
                dma(q, "qc1", d_inv[:, :], s_g3[:])
                q.wait_ge(S["qc1"], C["qc1"])
                src = d_inv.rearrange("n o -> o n").broadcast_to([128, 3])
                dma(q, "qc1", s_invb[:], src)
            invb_ld = C["qc1"]

        stA = ExitStack()
        sbA = lambda nm, sh, dt=F32: stA.enter_context(nc.sbuf_tensor(nm, sh, dt))
        s_winT = sbA("s_winT", [128, 16 * 8 * 128], BF16)
        s_wxT = sbA("s_wxT", [128, 4 * 160], BF16)
        s_w1qT = sbA("s_w1qT", [128, 16 * HSL], BF16)
        s_w2qT = sbA("s_w2qT", [128, 16 * HSL], BF16)

        with ExitStack() as stP:
            sbP = lambda nm, sh, dt=F32: stP.enter_context(nc.sbuf_tensor(nm, sh, dt))
            s_stg = [sbP(f"s_stgp{i}", [128, DM]) for i in range(2)]
            s_scrQ = sbP("s_scrQp", [128, DM])
            s_qbf = sbP("s_qbf", [128, DM], BF16)
            s_as3 = [sbP(f"s_as3{i}", [128, DM], BF16) for i in range(6)]
            s_aso = [sbP(f"s_aso{i}", [128, DM], BF16) for i in range(4)]

            last_qv = [0]
            last_tp = [0]

            def quant_tiles(dsrc, rows_list, ncols, kidx, emit_tp):
                r0 = 0
                for i, rr in enumerate(rows_list):
                    par = i % 2
                    stg = s_stg[par]

                    @blk.sync
                    def _(q, par=par, stg=stg, r0=r0, rr=rr, dsrc=dsrc, ncols=ncols):
                        if stg_gate[par]:
                            q.wait_ge(S[stg_gate[par][0]], stg_gate[par][1])
                        q.wait_ge(S["qc1"], invb_ld)
                        dma(q, f"qw{par}", stg[0:rr, 0:ncols], dsrc[r0:r0 + rr, :])
                    ld = C[f"qw{par}"]

                    @blk.scalar
                    def _(s, stg=stg, rr=rr, ncols=ncols, kidx=kidx, ld=ld, par=par):
                        s.wait_ge(S[f"qw{par}"], ld)
                        if last_qv[0]:
                            s.wait_ge(S["sv"], last_qv[0])
                        i2 = s.activation(s_scrQ[0:rr, 0:ncols], stg[0:rr, 0:ncols],
                                          AF.Identity, scale=s_invb[0:rr, kidx:kidx + 1],
                                          bias=MAGIC)
                        stg_gate[par] = ("ss", ms(i2, "ss"))
                    q1 = C["ss"]

                    @blk.vector
                    def _(v, rr=rr, ncols=ncols, q1=q1, i=i):
                        v.wait_ge(S["ss"], q1)
                        if last_tp[0]:
                            v.wait_ge(S["st"], last_tp[0])
                        i2 = v.tensor_scalar(s_scrQ[0:rr, 0:ncols], s_scrQ[0:rr, 0:ncols],
                                             MAGIC, 1.0, ALU.subtract, ALU.min)
                        ms(i2, "sv"); v.wait_ge(S["sv"], C["sv"])
                        i2 = v.tensor_scalar(s_qbf[0:rr, 0:ncols], s_scrQ[0:rr, 0:ncols],
                                             -1.0, None, ALU.max)
                        last_qv[0] = ms(i2, "sv")
                    emit_tp(i, rr, C["sv"])
                    last_tp[0] = C["st"]
                    r0 += rr

            for kidx, dstT in ((0, s_w1qT), (1, s_w2qT)):
                def tp(i, rr, qv, dstT=dstT):
                    for ki in range(16):
                        pe_transpose(s_qbf[0:rr, ki * 128:(ki + 1) * 128], rr, 128,
                                     dstT[:, ki * HSL + i * 128:ki * HSL + i * 128 + rr],
                                     wait_sv=qv)
                quant_tiles(d_w1 if kidx == 0 else d_w2, W1T, DM, kidx, tp)

            def tp3(i, rr, qv):
                for hb in range(6):
                    cols = W1T[hb]
                    pe_transpose(s_qbf[0:128, hb * 128:hb * 128 + cols], 128, cols,
                                 s_as3[hb][0:cols, i * 128:(i + 1) * 128], wait_sv=qv)
            quant_tiles(d_w3, [128] * 16, HSL, 2, tp3)
            w3t = C["sv"]

            @blk.sync
            def _(q):
                q.wait_ge(S["sv"], w3t)
                r0 = 0
                for hb in range(6):
                    rr = W1T[hb]
                    dma(q, "qc2", d_w3qt_sl[r0:r0 + rr, :], s_as3[hb][0:rr, :])
                    r0 += rr
            w3st = C["qc2"]

            @blk.gpsimd
            def _(g):
                g.wait_ge(S["qc2"], w3st)
                g.collective_compute("AllGather", ALU.bypass, replica_groups=RG,
                                     ins=[d_w3qt_sl.ap().opt()],
                                     outs=[d_w3qt.ap().opt()]).then_inc(S["cw3"], 1)

            for ki in range(16):
                par = ki % 2
                stg = s_stg[par]

                @blk.sync
                def _(q, par=par, stg=stg, ki=ki):
                    if stg_gate[par]:
                        q.wait_ge(S[stg_gate[par][0]], stg_gate[par][1])
                    dma(q, f"qw{par}", stg[:, 0:DSL], d_wo[ki * 128:(ki + 1) * 128, :])
                ld = C[f"qw{par}"]

                @blk.vector
                def _(v, stg=stg, ld=ld, ki=ki, par=par, tp=C["st"]):
                    v.wait_ge(S[f"qw{par}"], ld)
                    if tp:
                        v.wait_ge(S["st"], tp)
                    i2 = v.tensor_copy(s_qbf[:, 0:DSL], stg[:, 0:DSL])
                    stg_gate[par] = ("sv", ms(i2, "sv"))
                qv = C["sv"]
                for db in range(4):
                    pe_transpose(s_qbf[:, db * 128:(db + 1) * 128], 128, 128,
                                 s_aso[db][:, ki * 128:(ki + 1) * 128], wait_sv=qv)
            wot = C["sv"]

            @blk.sync
            def _(q):
                q.wait_ge(S["sv"], wot)
                for db in range(4):
                    dma(q, "qc3", d_wot_sl[db * 128:(db + 1) * 128, :], s_aso[db][:])
            wost = C["qc3"]

            @blk.gpsimd
            def _(g):
                g.wait_ge(S["qc3"], wost)
                g.collective_compute("AllGather", ALU.bypass, replica_groups=RG,
                                     ins=[d_wot_sl.ap().opt()],
                                     outs=[d_wot.ap().opt()]).then_inc(S["cwo"], 1)

            for eo in range(8):
                par = eo % 2
                stg = s_stg[par]

                @blk.sync
                def _(q, par=par, stg=stg, eo=eo):
                    if stg_gate[par]:
                        q.wait_ge(S[stg_gate[par][0]], stg_gate[par][1])
                    dma(q, f"qw{par}", stg[:], d_win[eo * 128:(eo + 1) * 128, :])
                ld = C[f"qw{par}"]

                @blk.vector
                def _(v, stg=stg, ld=ld, eo=eo, par=par, tp=C["st"]):
                    v.wait_ge(S[f"qw{par}"], ld)
                    if tp:
                        v.wait_ge(S["st"], tp)
                    i2 = v.tensor_copy(s_qbf[:], stg[:])
                    stg_gate[par] = ("sv", ms(i2, "sv"))
                qv = C["sv"]
                for ki in range(16):
                    pe_transpose(s_qbf[:, ki * 128:(ki + 1) * 128], 128, 128,
                                 s_winT[:, (ki * 8 + eo) * 128:(ki * 8 + eo) * 128 + 128],
                                 wait_sv=qv)

            @blk.sync
            def _(q):
                for par in range(2):
                    if stg_gate[par]:
                        q.wait_ge(S[stg_gate[par][0]], stg_gate[par][1])
                dma(q, "qw0", s_stg[0][0:128, 0:DSL], d_wx[0:128, :])
                dma(q, "qw1", s_stg[1][0:32, 0:DSL], d_wx[128:160, :])
            ldx0, ldx1 = C["qw0"], C["qw1"]

            @blk.vector
            def _(v, tp=C["st"]):
                v.wait_ge(S["qw0"], ldx0)
                v.wait_ge(S["qw1"], ldx1)
                if tp:
                    v.wait_ge(S["st"], tp)
                v.tensor_copy(s_qbf[0:128, 0:DSL], s_stg[0][0:128, 0:DSL])
                i2 = v.tensor_copy(s_qbf[0:32, DSL:2 * DSL], s_stg[1][0:32, 0:DSL])
                stg_gate[0] = stg_gate[1] = ("sv", ms(i2, "sv"))
            qv = C["sv"]
            for ki in range(4):
                pe_transpose(s_qbf[0:128, ki * 128:(ki + 1) * 128], 128, 128,
                             s_wxT[:, ki * 160:ki * 160 + 128], wait_sv=qv)
                pe_transpose(s_qbf[0:32, DSL + ki * 128:DSL + (ki + 1) * 128], 32, 128,
                             s_wxT[:, ki * 160 + 128:ki * 160 + 160], wait_sv=qv)

            @blk.sync
            def _(q):
                if stg_gate[0]:
                    q.wait_ge(S[stg_gate[0][0]], stg_gate[0][1])
                dma(q, "qw0", s_stg[0][0:128, 0:512].rearrange("p (d r) -> p d r", d=4),
                    d_wdt.rearrange("(d p) r -> p d r", p=128))
            lddt = C["qw0"]

            @blk.vector
            def _(v, tp=C["st"]):
                v.wait_ge(S["qw0"], lddt)
                if tp:
                    v.wait_ge(S["st"], tp)
                i2 = v.tensor_copy(s_qbf[0:128, 0:512], s_stg[0][0:128, 0:512])
                ms(i2, "sv")
            qv2 = C["sv"]
            for dt in range(4):
                pe_transpose(s_qbf[0:128, dt * 128:(dt + 1) * 128], 128, 128,
                             s_wdtT[:, dt * 128:(dt + 1) * 128], wait_sv=qv2)

        set_free(pT0, "sv", tp_state["last"][0])
        set_free(pT1, "sv", tp_state["last"][1])
        emit_barrier()

        s_xp = [sbA(f"s_xp{i}", [128, DM]) for i in range(5)]
        s_xnb = [sbA(f"s_xnb{i}", [128, DM], BF16) for i in range(2)]
        s_scrA = sbA("s_scrA", [128, DM])
        s_stats = sbA("s_stats", [128, 16])
        s_xnT = [sbA(f"s_xnT{i}", [128, 16 * TA], BF16) for i in range(2)]
        s_xh = sbA("s_xh", [128, 4 * (TA + 3)])
        s_xc = sbA("s_xc", [128, 4 * TA])
        s_xcb = sbA("s_xcb", [128, 4 * TA], BF16)
        s_szb = sbA("s_szb", [128, 4 * TA])
        s_xd0 = sbA("s_xd0", [128, TA])
        s_xd1 = sbA("s_xd1", [32, TA])
        s_sig = sbA("s_sig", [128, TA])
        s_hbf = [sbA(f"s_hbf{i}", [128, TA], BF16) for i in range(2)]

        @blk.vector
        def _(v):
            i = v.memset(s_xh[:], 0.0)
            ms(i, "sv")
        halo0 = C["sv"]

        xload = {}
        norm_done = {}
        tpgrp_done = {}
        stats_gate = 0
        ffn_last_par = [0, 0]
        evx_done = {}
        tail_done = {}
        xcdma = {}
        zdma = {}
        xcb_free = 0
        zmm = {}
        zev_c = [0, 0, 0, 0]
        last_h = [None]
        hid_prev = [0]

        nmr_of = {}

        def emit_stats(c):
            nonlocal stats_gate
            for pt in range(4):
                @blk.sync
                def _(q, c=c, pt=pt):
                    if c * 4 + pt >= 5:
                        q.wait_ge(S["sv"], norm_done[c * 4 + pt - 5])
                    elif c * 4 + pt == 4:
                        q.wait_ge(S["sv"], norm_done[0])
                    dma(q, f"qx{pt}", s_xp[(c * 4 + pt) % 5][:],
                        d_x[c * TA + pt * 128:c * TA + (pt + 1) * 128, :])
                xload[(c, pt)] = C[f"qx{pt}"]

            @blk.scalar
            def _(s, c=c):
                if stats_gate:
                    s.wait_ge(S["sv"], stats_gate)
                for pt in range(4):
                    s.wait_ge(S[f"qx{pt}"], xload[(c, pt)])
                    if pt > 0 or c > 0:
                        s.wait_ge(S["ss"], C["ss"])
                    i = s.activation(s_scrA[:], s_xp[(c * 4 + pt) % 5][:], AF.Square,
                                     accum_out=s_stats[:, pt:pt + 1])
                    ms(i, "ss")
            stats_done = C["ss"]

            @blk.vector
            def _(v, c=c):
                if stats_gate:
                    v.wait_ge(S["sv"], stats_gate)
                for pt in range(4):
                    v.wait_ge(S[f"qx{pt}"], xload[(c, pt)])
                    i = v.tensor_reduce(s_stats[:, 4 + pt:5 + pt], s_xp[(c * 4 + pt) % 5][:],
                                        AX.X, ALU.add)
                    ms(i, "sv")
            sums_done_p = C["sv"]

            @blk.vector
            def _(v, stats_done=stats_done, sums_done_p=sums_done_p):
                v.wait_ge(S["ss"], stats_done)
                v.wait_ge(S["sv"], sums_done_p)
                i = v.tensor_scalar(s_stats[:, 4:8], s_stats[:, 4:8], -1.0 / DM, None,
                                    ALU.mult)
                ms(i, "sv"); v.wait_ge(S["sv"], C["sv"])
                i = v.tensor_scalar(s_stats[:, 0:4], s_stats[:, 0:4], 1.0 / DM, None,
                                    ALU.mult)
                ms(i, "sv"); v.wait_ge(S["sv"], C["sv"])
                i = v.tensor_tensor(s_stats[:, 8:12], s_stats[:, 4:8], s_stats[:, 4:8],
                                    ALU.mult)
                ms(i, "sv"); v.wait_ge(S["sv"], C["sv"])
                i = v.tensor_tensor(s_stats[:, 0:4], s_stats[:, 0:4],
                                    s_stats[:, 8:12], ALU.subtract)
                ms(i, "sv")
            var_done = C["sv"]

            @blk.scalar
            def _(s, var_done=var_done):
                s.wait_ge(S["sv"], var_done)
                i = s.activation(s_stats[:, 8:12], s_stats[:, 0:4], AF.Ln, bias=EPS_LN)
                ms(i, "ss"); s.wait_ge(S["ss"], C["ss"])
                i = s.activation(s_stats[:, 8:12], s_stats[:, 8:12], AF.Exp, scale=-0.5)
                ms(i, "ss")
            rstd_done = C["ss"]

            @blk.vector
            def _(v, rstd_done=rstd_done):
                v.wait_ge(S["ss"], rstd_done)
                i = v.tensor_tensor(s_stats[:, 12:16], s_stats[:, 4:8],
                                    s_stats[:, 8:12], ALU.mult)
                ms(i, "sv")
            nmr_of[c] = C["sv"]
            stats_gate = C["sv"]

        emit_stats(0)
        for c in range(NCA):
            nmr_done = nmr_of[c]
            for pt in range(4):
                gi = c * 4 + pt

                @blk.vector
                def _(v, pt=pt, gi=gi, nmr_done=nmr_done):
                    v.wait_ge(S["sv"], nmr_done)
                    if gi >= 2 and (gi - 2) in tpgrp_done:
                        v.wait_ge(S["st"], tpgrp_done[gi - 2])
                    nmr_b = s_stats[:, 12 + pt:13 + pt].broadcast_to([128, DM])
                    i = v.scalar_tensor_tensor(s_xnb[pt % 2][:], s_xp[(c * 4 + pt) % 5][:],
                                               s_stats[:, 8 + pt:9 + pt], nmr_b,
                                               ALU.mult, ALU.add)
                    norm_done[gi] = ms(i, "sv")

                for g4 in range(4):
                    par = tp_state["par"] % 2
                    tp_state["par"] += 1
                    pT = pT0 if par == 0 else pT1

                    @blk.tensor
                    def _(t, pT=pT, pt=pt, g4=g4, gi=gi, par=par):
                        t.wait_ge(S["sv"], norm_done[gi])
                        if tp_state["last"][par]:
                            t.wait_ge(S["sv"], tp_state["last"][par])
                        for j in range(4):
                            ki = g4 * 4 + j
                            i2 = t.transpose(pT[0:128, j * 128:(j + 1) * 128],
                                             s_xnb[pt % 2][:, ki * 128:(ki + 1) * 128],
                                             s_id[:])
                            if j == 3:
                                ms(i2, "st")
                    tpgrp_done[gi] = C["st"]
                    tdone = C["st"]

                    @blk.vector
                    def _(v, pT=pT, pt=pt, g4=g4, tdone=tdone, par=par, c=c,
                          flm=None):
                        v.wait_ge(S["st"], tdone)
                        if ffn_last_par[c % 2]:
                            v.wait_ge(S["st"], ffn_last_par[c % 2])
                        dst = s_xnT[c % 2][:].rearrange("p (k t) -> p k t", k=16)[
                            :, g4 * 4:(g4 + 1) * 4, pt * 128:(pt + 1) * 128]
                        i2 = v.tensor_copy(dst, pT[:, 0:512].rearrange("p (k t) -> p k t", k=4))
                        tp_state["last"][par] = ms(i2, "sv")
            xnT_done = C["sv"]

            if c + 1 < NCA:
                emit_stats(c + 1)

            PSROT = [pA, pB, pC2, pD]
            for eo in range(8):
                p = PSROT[eo % 4]

                @blk.tensor
                def _(t, c=c, eo=eo, p=p, xnT_done=xnT_done):
                    t.wait_ge(S["sv"], xnT_done)
                    t_guard(t, p)
                    for ki in range(16):
                        i2 = t.matmul(p[:], s_winT[:, (ki * 8 + eo) * 128:
                                                   (ki * 8 + eo + 1) * 128],
                                      s_xnT[c % 2][:, ki * TA:ki * TA + 512],
                                      start=(ki == 0), stop=(ki == 15))
                        if ki == 15:
                            ms(i2, "st")
                mmdone = C["st"]

                if eo < 4:
                    dt = eo

                    @blk.vector
                    def _(v, c=c, dt=dt, p=p, mmdone=mmdone):
                        v.wait_ge(S["st"], mmdone)
                        if c == 0:
                            v.wait_ge(S["sv"], halo0)
                        i2 = v.tensor_copy(s_xh[:, dt * (TA + 3) + 3:
                                               dt * (TA + 3) + 3 + TA], p[:])
                        evx_done[(c, dt)] = ms(i2, "sv")
                    set_free(p, "sv", evx_done[(c, dt)])
                else:
                    dt = eo - 4
                    zmm[dt] = (mmdone, p)

            @blk.scalar
            def _(s, c=c, zmm=dict(zmm)):
                for dt in range(4):
                    mmdone, p = zmm[dt]
                    s.wait_ge(S["st"], mmdone)
                    if (c - 1, dt) in zdma:
                        s.wait_ge(S[f"qz{dt}"], zdma[(c - 1, dt)])
                    i2 = s.activation(s_szb[:, dt * TA:(dt + 1) * TA], p[:], AF.Silu)
                    ms(i2, "ss")
                    zev_c[dt] = C["ss"]
                    set_free(p, "ss", C["ss"])

            for dt in range(4):
                @blk.sync
                def _(q, c=c, dt=dt, zev=zev_c[dt]):
                    q.wait_ge(S["ss"], zev)
                    dma(q, f"qz{dt}", d_sz[dt * 128:(dt + 1) * 128, c * TA:(c + 1) * TA],
                        s_szb[:, dt * TA:(dt + 1) * TA])
                zdma[(c, dt)] = C[f"qz{dt}"]

            for dt in range(4):
                @blk.vector
                def _(v, c=c, dt=dt):
                    v.wait_ge(S["sv"], evx_done[(c, dt)])
                    if (c - 1, dt) in xcdma:
                        v.wait_ge(S[f"qxc{dt}"], xcdma[(c - 1, dt)])
                    base = dt * (TA + 3)
                    xs = s_xc[:, dt * TA:(dt + 1) * TA]
                    i2 = v.tensor_scalar(xs, s_xh[:, base:base + TA],
                                         s_cw[:, dt * 4:dt * 4 + 1], None, ALU.mult)
                    for k in range(1, 4):
                        ms(i2, "sv"); v.wait_ge(S["sv"], C["sv"])
                        i2 = v.scalar_tensor_tensor(xs, s_xh[:, base + k:base + k + TA],
                                                    s_cw[:, dt * 4 + k:dt * 4 + k + 1],
                                                    xs, ALU.mult, ALU.add)
                    ms(i2, "sv"); v.wait_ge(S["sv"], C["sv"])
                    i2 = v.tensor_copy(s_xh[:, base:base + 3],
                                       s_xh[:, base + TA:base + TA + 3])
                    tail_done[(c, dt)] = ms(i2, "sv")

                @blk.scalar
                def _(s, c=c, dt=dt):
                    s.wait_ge(S["sv"], tail_done[(c, dt)])
                    xs = s_xc[:, dt * TA:(dt + 1) * TA]
                    i2 = s.activation(xs, xs, AF.Silu, bias=s_cb[:, dt:dt + 1])
                    ms(i2, "ss")
                silu_done = C["ss"]

                @blk.vector
                def _(v, c=c, dt=dt, silu_done=silu_done):
                    v.wait_ge(S["ss"], silu_done)
                    if xcb_free:
                        v.wait_ge(S["st"], xcb_free)
                    i2 = v.tensor_copy(s_xcb[:, dt * TA:(dt + 1) * TA],
                                       s_xc[:, dt * TA:(dt + 1) * TA])
                    ms(i2, "sv")
                xcbf = C["sv"]

                @blk.sync
                def _(q, c=c, dt=dt, sd=silu_done):
                    q.wait_ge(S["ss"], sd)
                    dma(q, f"qxc{dt}", d_xc[dt * 128:(dt + 1) * 128, c * TA:(c + 1) * TA],
                        s_xc[:, dt * TA:(dt + 1) * TA])
                xcdma[(c, dt)] = C[f"qxc{dt}"]
            xcb_done = C["sv"]

            for ho in range(6):
                rr = W1T[ho]

                @blk.tensor
                def _(t, c=c, ho=ho, rr=rr, xnT_done=xnT_done):
                    t_guard(t, pA)
                    for ki in range(16):
                        i2 = t.matmul(pA[0:rr, :],
                                      s_w1qT[:, ki * HSL + ho * 128:
                                             ki * HSL + ho * 128 + rr],
                                      s_xnT[c % 2][:, ki * TA:ki * TA + 512],
                                      start=(ki == 0), stop=(ki == 15))
                        if ki == 15:
                            ms(i2, "st")
                x1mm = C["st"]

                @blk.tensor
                def _(t, c=c, ho=ho, rr=rr):
                    t_guard(t, pB)
                    for ki in range(16):
                        i2 = t.matmul(pB[0:rr, :],
                                      s_w2qT[:, ki * HSL + ho * 128:
                                             ki * HSL + ho * 128 + rr],
                                      s_xnT[c % 2][:, ki * TA:ki * TA + 512],
                                      start=(ki == 0), stop=(ki == 15))
                        if ki == 15:
                            ms(i2, "st")
                x2mm = C["st"]
                if ho == 5:
                    ffn_last_par[c % 2] = x2mm

                @blk.scalar
                def _(s, rr=rr, x1mm=x1mm):
                    s.wait_ge(S["st"], x1mm)
                    if hid_prev[0]:
                        s.wait_ge(S["sv"], hid_prev[0])
                    i2 = s.activation(s_sig[0:rr, :], pA[0:rr, :], AF.Sigmoid)
                    ms(i2, "ss")
                sgd = C["ss"]
                set_free(pA, "ss", sgd)

                @blk.vector
                def _(v, c=c, ho=ho, rr=rr, sgd=sgd, x2mm=x2mm):
                    v.wait_ge(S["ss"], sgd)
                    v.wait_ge(S["st"], x2mm)
                    if last_h[0] is not None:
                        v.wait_ge(S[last_h[0][0]], last_h[0][1])
                    i2 = v.tensor_tensor(s_hbf[ho % 2][0:rr, :], s_sig[0:rr, :],
                                         pB[0:rr, :], ALU.mult)
                    hid_prev[0] = ms(i2, "sv")
                hid = C["sv"]
                set_free(pB, "sv", hid)

                @blk.sync
                def _(q, c=c, ho=ho, rr=rr, hid=hid):
                    q.wait_ge(S["sv"], hid)
                    dma(q, f"chs{c // 2}", d_hin[c // 2][ho * 128:ho * 128 + rr,
                                               (c % 2) * TA:(c % 2 + 1) * TA],
                        s_hbf[ho % 2][0:rr, :])
                last_h[0] = (f"chs{c // 2}", C[f"chs{c // 2}"])

            @blk.tensor
            def _(t, c=c, xcb_done=xcb_done):
                t.wait_ge(S["sv"], xcb_done)
                t_guard(t, pA)
                for ki in range(4):
                    i2 = t.matmul(pA[:], s_wxT[:, ki * 160:ki * 160 + 128],
                                  s_xcb[:, ki * TA:ki * TA + 512],
                                  start=(ki == 0), stop=(ki == 3))
                    if ki == 3:
                        ms(i2, "st")
                t_guard(t, pW32)
                for ki in range(4):
                    i2 = t.matmul(pW32[:], s_wxT[:, ki * 160 + 128:ki * 160 + 160],
                                  s_xcb[:, ki * TA:ki * TA + 512],
                                  start=(ki == 0), stop=(ki == 3))
                    if ki == 3:
                        ms(i2, "st")
            wx_done = C["st"]
            xcb_free = wx_done

            @blk.scalar
            def _(s, c=c, wx_done=wx_done):
                s.wait_ge(S["st"], wx_done)
                if c > 0:
                    s.wait_ge(S[f"cxs{c - 1}"], 32)
                s.copy(s_xd0[:], pA[:])
                i2 = s.copy(s_xd1[:], pW32[:])
                ms(i2, "ss")
            wx_ev = C["ss"]
            set_free(pA, "ss", wx_ev)
            set_free(pW32, "ss", wx_ev)

            @blk.sync
            def _(q, c=c, wx_ev=wx_ev):
                q.wait_ge(S["ss"], wx_ev)
                co = (c % 4) * TA
                dma(q, f"cxs{c}", d_xdp[c // 4][0:128, co:co + TA], s_xd0[:])
                dma(q, f"cxs{c}", d_xdp[c // 4][128:160, co:co + TA], s_xd1[:])

            if c % 4 == 3:
                @blk.gpsimd
                def _(g, c=c):
                    for i in range(4):
                        g.wait_ge(S[f"cxs{c - 3 + i}"], 32)
                    gi = c // 4
                    g.collective_compute("AllReduce", ALU.add, replica_groups=RG,
                                         ins=[d_xdp[gi].ap().opt()],
                                         outs=[d_xdr[gi].ap().opt()]
                                         ).then_inc(S[f"cx{gi}"], 1)

            if c % 2 == 1:
                b = c // 2

                @blk.gpsimd
                def _(g, b=b, w=C[f"chs{c // 2}"]):
                    g.wait_ge(S[f"chs{b}"], w)
                    g.collective_compute(
                        "AllGather", ALU.bypass, replica_groups=RG,
                        ins=[d_hin[b].ap().opt()],
                        outs=[d_hcat.ap()[b * HF:(b + 1) * HF, :].opt()]
                    ).then_inc(S[f"ch{b}"], 1)

        stA.close()
        emit_barrier()

        stB = ExitStack()
        sbB = lambda nm, sh, dt=F32: stB.enter_context(nc.sbuf_tensor(nm, sh, dt))
        s_dA = [sbB(f"s_dA{i}", [128, 16 * TB]) for i in range(2)]
        s_bb = [sbB(f"s_bb{i}", [128, 16 * TB]) for i in range(2)]
        s_dtc = [sbB(f"s_dtc{i}", [128, TB], BF16) for i in range(2)]
        s_brep = sbB("s_brep", [128, 16 * TB], BF16)
        s_crep = sbB("s_crep", [128, 16 * TB], BF16)
        s_delta = sbB("s_delta", [128, 4 * TB])
        s_xcB = [sbB(f"s_xcB{i}", [128, TB]) for i in range(5)]
        s_szB = [sbB(f"s_szB{i}", [128, TB]) for i in range(5)]
        s_dU = sbB("s_dU", [128, 4 * TB])
        s_ys = sbB("s_ys", [128, TB])
        s_ybf = [sbB(f"s_ybf{i}", [128, TB], BF16) for i in range(2)]
        s_car = sbB("s_car", [128, 64])
        s_t16 = sbB("s_t16", [128, 16])
        s_ones = sbB("s_ones", [1, 128], BF16)
        s_bcrow = sbB("s_bcrow", [1, 2 * TB], BF16)

        @blk.gpsimd
        def _(g):
            for gi in range(4):
                g.wait_ge(S[f"cx{gi}"], 1)
            for gi in range(4):
                dma(g, "qxr", s_dA[0][:, gi * 4 * TB:(gi + 1) * 4 * TB],
                    d_xdr[gi][0:128, :])
                dma(g, "qxr", s_bb[0][0:32, gi * 4 * TB:(gi + 1) * 4 * TB],
                    d_xdr[gi][128:160, :])
        xdr_ld = C["qxr"]

        @blk.vector
        def _(v):
            v.wait_ge(S["qxr"], xdr_ld)
            v.memset(s_car[:], 0.0)
            i = v.tensor_copy(s_brep[:], s_dA[0][:])
            ms(i, "sv")
        prol_dt = C["sv"]

        @blk.scalar
        def _(s):
            s.wait_ge(S["sv"], prol_dt)
            dma(s, "qbc", d_dtb[:, :], s_brep[:])
        dtb_st = C["qbc"]

        @blk.vector
        def _(v):
            v.wait_ge(S["qbc"], dtb_st)
            v.wait_ge(S["sv"], C["sv"])
            i = v.tensor_copy(s_brep[0:32, 0:L], s_bb[0][0:32, :])
            ms(i, "sv")
        prol = C["sv"]

        @blk.scalar
        def _(s):
            s.wait_ge(S["sv"], prol)
            dma(s, "qbc", d_bc16[:, :], s_brep[0:32, 0:L])
        bc_st = C["qbc"]

        @blk.vector
        def _(v):
            i = v.memset(s_ones[:], 1.0)
            ms(i, "sv")
        ones_done = C["sv"]

        brep_done = [0] * NCB
        crep_done = [0] * NCB
        row_mm = [0]

        def emit_bcast(sc, which, b_done, prod_done):
            dst = s_brep if which == 0 else s_crep
            rows = (0, 16) if which == 0 else (16, 32)

            for h in range(8):
                @blk.sync
                def _(q, sc=sc, rows=rows, h=h):
                    q.wait_ge(S["qbc"], bc_st)
                    q.wait_ge(S["sv"], ones_done)
                    if row_mm[0]:
                        q.wait_ge(S["st"], row_mm[0])
                    src = d_bc16[rows[0] + 2 * h:rows[0] + 2 * h + 2,
                                 sc * TB:(sc + 1) * TB] \
                        .rearrange("(o n) t -> o n t", o=1)
                    dma(q, "qbc", s_bcrow[:].rearrange("o (n t) -> o n t", n=2), src)
                row_ld = C["qbc"]

                mmj2 = [0, 0]
                for j2 in range(2):
                    j = 2 * h + j2
                    p = pC2 if j % 2 == 0 else pD

                    @blk.tensor
                    def _(t, sc=sc, j2=j2, p=p, w=row_ld):
                        t.wait_ge(S["qbc"], w)
                        t_guard(t, p)
                        i2 = t.matmul(p[:], s_ones[:],
                                      s_bcrow[:, j2 * TB:(j2 + 1) * TB],
                                      start=True, stop=True)
                        i3 = ms(i2, "st")
                        mmj2[j2] = i3
                        if j2 == 1:
                            row_mm[0] = i3

                @blk.scalar
                def _(s, sc=sc, h=h, mmj2=list(mmj2), dst=dst, which=which):
                    for j2 in range(2):
                        j = 2 * h + j2
                        p = pC2 if j % 2 == 0 else pD
                        s.wait_ge(S["st"], mmj2[j2])
                        if j == 0 and sc > 0:
                            if which == 0:
                                s.wait_ge(S["sv"], b_done[sc * 4 - 1])
                            else:
                                s.wait_ge(S["sv"], prod_done[sc * 4 - 1])
                        i2 = s.copy(dst[:, j * TB:(j + 1) * TB], p[:])
                        ms(i2, "ss")
                        set_free(p, "ss", C["ss"])
            if which == 0:
                brep_done[sc] = C["ss"]
            else:
                crep_done[sc] = C["ss"]

        exp_done = [0] * (NCB * 4)
        b_done = [0] * (NCB * 4)
        it_done = [0] * (NCB * 4)
        last_y = [None]
        du_done = [0] * NCB
        delta_done = [0] * NCB
        ld_xcsz = [0] * NCB
        ld_bc = [0] * NCB
        wdt_mm = [0] * NCB

        scan_done = [0] * (NCB * 4)
        prod_done = [0] * (NCB * 4)
        tree_done = [0] * (NCB * 4)

        def emit_dve_tail(itp):
            scp = itp // 4
            dtp = itp % 4
            parp = itp % 2
            v = _cur_v[0]
            v.wait_ge(S["sv"], prod_done[itp])
            if last_y[0] is not None:
                v.wait_ge(S[last_y[0][0]], last_y[0][1])

            def step(i2, v=v):
                ms(i2, "sv"); v.wait_ge(S["sv"], C["sv"])
            for half in (8, 4, 2):
                step(v.tensor_tensor(s_dA[parp][:, 0:half * TB],
                                     s_dA[parp][:, 0:half * TB],
                                     s_dA[parp][:, half * TB:2 * half * TB],
                                     ALU.add))
            step(v.tensor_tensor(s_ys[:], s_dA[parp][:, 0:TB],
                                 s_dA[parp][:, TB:2 * TB], ALU.add))
            step(v.scalar_tensor_tensor(s_ys[:],
                                        s_xcB[itp % 5][:],
                                        s_dd[:, dtp:dtp + 1], s_ys[:],
                                        ALU.mult, ALU.add))
            i2 = v.tensor_tensor(s_ybf[parp][:], s_ys[:],
                                 s_szB[itp % 5][:], ALU.mult)
            tree_done[itp] = it_done[itp] = ms(i2, "sv")

        def emit_ystore(itp):
            scp = itp // 4
            dtp = itp % 4
            parp = itp % 2

            @blk.sync
            def _(q, scp=scp, dtp=dtp, parp=parp, w=it_done[itp]):
                q.wait_ge(S["sv"], w)
                dma(q, f"cys{scp // 2}", d_yin[scp // 2][dtp * 128:(dtp + 1) * 128,
                                            (scp % 2) * TB:(scp % 2 + 1) * TB],
                    s_ybf[parp][:])
            last_y[0] = (f"cys{scp // 2}", C[f"cys{scp // 2}"])

        _cur_v = [None]
        emit_bcast(0, 0, b_done, prod_done)
        emit_bcast(0, 1, b_done, prod_done)
        for sc in range(NCB):
            @blk.sync
            def _(q, sc=sc):
                q.wait_ge(S["sv"], prol)
                for dt in range(4):
                    it = sc * 4 + dt
                    if it >= 5:
                        q.wait_ge(S["sv"], it_done[it - 5])
                    dma(q, "qbx", s_xcB[it % 5][:],
                        d_xc[dt * 128:(dt + 1) * 128, sc * TB:(sc + 1) * TB])
                    dma(q, "qbx", s_szB[it % 5][:],
                        d_sz[dt * 128:(dt + 1) * 128, sc * TB:(sc + 1) * TB])
            ld_xcsz[sc] = C["qbx"]

            @blk.sync
            def _(q, sc=sc):
                q.wait_ge(S["qbc"], bc_st)
                if sc >= 2 and wdt_mm[sc - 2]:
                    q.wait_ge(S["st"], wdt_mm[sc - 2])
                dma(q, f"qdt{sc % 2}", s_dtc[sc % 2][:], d_dtb[:, sc * TB:(sc + 1) * TB])
            dtc_ld = C[f"qdt{sc % 2}"]

            for dt in range(4):
                p = pA if dt % 2 == 0 else pB

                @blk.tensor
                def _(t, sc=sc, dt=dt, p=p, dtc_ld=dtc_ld):
                    t.wait_ge(S[f"qdt{sc % 2}"], dtc_ld)
                    t_guard(t, p)
                    i2 = t.matmul(p[:], s_wdtT[:, dt * 128:(dt + 1) * 128],
                                  s_dtc[sc % 2][:],
                                  start=True, stop=True)
                    i3 = ms(i2, "st")
                    if dt == 3:
                        wdt_mm[sc] = i3
                mmd = C["st"]

                @blk.scalar
                def _(s, sc=sc, dt=dt, p=p, mmd=mmd):
                    s.wait_ge(S["st"], mmd)
                    if sc > 0:
                        s.wait_ge(S["sv"], du_done[sc - 1])
                    ds = s_delta[:, dt * TB:(dt + 1) * TB]
                    i2 = s.activation(ds, p[:], AF.Exp, bias=s_bdt[:, dt:dt + 1])
                    ms(i2, "ss"); s.wait_ge(S["ss"], C["ss"])
                    i2 = s.activation(ds, ds, AF.Ln, bias=1.0)
                    ms(i2, "ss")
                delta_done[sc] = C["ss"]
                set_free(p, "ss", delta_done[sc])

            for dt in range(4):
                it = sc * 4 + dt
                par = it % 2

                @blk.scalar
                def _(s, sc=sc, dt=dt, it=it, par=par):
                    if it >= 2:
                        s.wait_ge(S["sv"], tree_done[it - 2])
                    else:
                        s.wait_ge(S["sv"], prol)
                    dv = s_dA[par][:].rearrange("p (n t) -> p n t", n=16)
                    i2 = None
                    for n in range(16):
                        i2 = s.activation(dv[:, n, :], s_delta[:, dt * TB:(dt + 1) * TB],
                                          AF.Exp, scale=s_aneg[:, dt * 16 + n:
                                                              dt * 16 + n + 1])
                    exp_done[it] = ms(i2, "ss")

                if it >= 1:
                    @blk.vector
                    def _(v, it=it, sc=sc):
                        v.wait_ge(S["ss"], crep_done[(it - 1) // 4])
                        v.wait_ge(S["sv"], scan_done[it - 1])
                        i2 = v.tensor_tensor(s_dA[(it - 1) % 2][:],
                                             s_dA[(it - 1) % 2][:], s_crep[:],
                                             ALU.mult)
                        prod_done[it - 1] = ms(i2, "sv")

                if dt == 0 and sc > 0:
                    emit_bcast(sc, 1, b_done, prod_done)

                if dt == 0:
                    if it >= 1:
                        @blk.vector
                        def _(v, it=it):
                            _cur_v[0] = v
                            emit_dve_tail(it - 1)
                        emit_ystore(it - 1)

                    @blk.vector
                    def _(v, sc=sc):
                        v.wait_ge(S["ss"], delta_done[sc])
                        v.wait_ge(S["qbx"], ld_xcsz[sc])
                        if sc > 0:
                            v.wait_ge(S["sv"], b_done[sc * 4 - 1])
                        i2 = None
                        for dt2 in range(4):
                            i2 = v.tensor_tensor(s_dU[:, dt2 * TB:(dt2 + 1) * TB],
                                                 s_delta[:, dt2 * TB:(dt2 + 1) * TB],
                                                 s_xcB[(sc * 4 + dt2) % 5][:],
                                                 ALU.mult)
                        ms(i2, "sv")
                    du_done[sc] = C["sv"]

                @blk.vector
                def _(v, sc=sc, dt=dt, it=it, par=par):
                    v.wait_ge(S["ss"], brep_done[sc])
                    v.wait_ge(S["sv"], du_done[sc])
                    dv = s_dA[par][:].rearrange("p (n t) -> p n t", n=16)

                    def step(i2, v=v):
                        ms(i2, "sv"); v.wait_ge(S["sv"], C["sv"])
                    dub = s_dU[:, dt * TB:(dt + 1) * TB] \
                        .rearrange("p (o t) -> p o t", o=1).broadcast_to([128, 16, TB])
                    step(v.tensor_tensor(s_bb[par][:].rearrange("p (n t) -> p n t", n=16),
                                         dub,
                                         s_brep[:].rearrange("p (n t) -> p n t", n=16),
                                         ALU.mult))
                    b_done[it] = C["sv"]
                    v.wait_ge(S["ss"], exp_done[it])
                    if sc > 0:
                        step(v.tensor_tensor(s_t16[:], dv[:, :, 0],
                                             s_car[:, dt * 16:(dt + 1) * 16], ALU.mult))
                        step(v.tensor_tensor(s_bb[par][:].rearrange(
                            "p (n t) -> p n t", n=16)[:, :, 0],
                            s_bb[par][:].rearrange("p (n t) -> p n t", n=16)[:, :, 0],
                            s_t16[:], ALU.add))
                        step(v.memset(dv[:, :, 0], 0.0))
                    step(v.tensor_tensor_scan(s_dA[par][:], s_dA[par][:], s_bb[par][:],
                                              0.0, ALU.mult, ALU.add))
                    i2 = v.tensor_copy(s_car[:, dt * 16:(dt + 1) * 16], dv[:, :, TB - 1])
                    scan_done[it] = ms(i2, "sv")

                if it >= 1 and dt > 0:
                    @blk.vector
                    def _(v, it=it):
                        _cur_v[0] = v
                        emit_dve_tail(it - 1)
                    emit_ystore(it - 1)

            if sc + 1 < NCB:
                emit_bcast(sc + 1, 0, b_done, prod_done)

            if sc >= 3 and sc % 2 == 1:
                b = (sc - 2) // 2

                @blk.gpsimd
                def _(g, b=b):
                    g.wait_ge(S[f"cys{b}"], 128)
                    g.collective_compute(
                        "AllGather", ALU.bypass, replica_groups=RG,
                        ins=[d_yin[b].ap().opt()],
                        outs=[d_ycat.ap()[b * DI:(b + 1) * DI, :].opt()]
                    ).then_inc(S[f"cy{b}"], 1)

        @blk.vector
        def _(v):
            v.wait_ge(S["sv"], scan_done[NCB * 4 - 1])
            i2 = v.tensor_tensor(s_dA[(NCB * 4 - 1) % 2][:],
                                 s_dA[(NCB * 4 - 1) % 2][:], s_crep[:], ALU.mult)
            prod_done[NCB * 4 - 1] = ms(i2, "sv")

        @blk.vector
        def _(v):
            _cur_v[0] = v
            emit_dve_tail(NCB * 4 - 1)
        emit_ystore(NCB * 4 - 1)
        for b in (6, 7):
            @blk.gpsimd
            def _(g, b=b):
                g.wait_ge(S[f"cys{b}"], 128)
                g.collective_compute(
                    "AllGather", ALU.bypass, replica_groups=RG,
                    ins=[d_yin[b].ap().opt()],
                    outs=[d_ycat.ap()[b * DI:(b + 1) * DI, :].opt()]
                ).then_inc(S[f"cy{b}"], 1)

        stB.close()
        emit_barrier()

        stC = ExitStack()
        sbC = lambda nm, sh, dt=F32: stC.enter_context(nc.sbuf_tensor(nm, sh, dt))
        s_g = sbC("s_g", [128, 4 * DM])
        s_s = sbC("s_s", [128, 4 * DM])
        s_borep = sbC("s_borep", [128, DM])

        @blk.sync
        def _(q):
            bo_b = d_bo.rearrange("(o x) -> o x", o=1).broadcast_to([128, DM])
            dma(q, "qbo", s_borep[:], bo_b)
        borep_ld = C["qbo"]

        rk_holder = {}

        @blk.gpsimd
        def _(g):
            g.wait_ge(S["cw3"], 1)
            g.wait_ge(S["cwo"], 1)
            for b in range(NCORE):
                g.wait_ge(S[f"ch{b}"], 1)
                g.wait_ge(S[f"cy{b}"], 1)
            rk_holder["rk"] = g.cc_rank(RG)

        outd = []
        last_cmb = [0]
        for cc in range(NCC):
            with ExitStack() as stw3:
                s_hidC = stw3.enter_context(nc.sbuf_tensor(f"s_hidC{cc}",
                                                           [128, NKH * TC], BF16))
                s_w3s = [stw3.enter_context(nc.sbuf_tensor(f"s_w3s{cc}_{i}",
                                                           [128, NKH * TC], BF16))
                         for i in range(2)]
                guard = {k: C[k] for k in ("st", "sv", "ss", "qo", "qcx0", "qcx1")}

                @blk.gpsimd
                def _(g, cc=cc, guard=guard):
                    for k, vv in guard.items():
                        if vv:
                            g.wait_ge(S[k], vv)
                    rk = rk_holder["rk"]
                    src = d_hcat.ap()[bass.ts(rk, HF), cc * TC:(cc + 1) * TC] \
                        .rearrange("(k p) t -> p k t", p=128)
                    dstv = s_hidC[:].rearrange("p (k t) -> p k t", k=NKH)
                    for i in range(4):
                        ks = (NKH * i) // 4
                        ke = (NKH * (i + 1)) // 4
                        dma(g, "qgh", dstv[:, ks:ke], src[:, ks:ke])
                hid_ld = C["qgh"]

                strip_free = [0, 0]
                for mb in range(4):
                    spar = mb % 2

                    @blk.gpsimd
                    def _(g, cc=cc, mb=mb, spar=spar):
                        if strip_free[spar]:
                            g.wait_ge(S["st"], strip_free[spar])
                        src = d_w3qt.ap()[:, mb * TC:(mb + 1) * TC] \
                            .rearrange("(k p) t -> p k t", p=128)
                        dstv = s_w3s[spar][:].rearrange("p (k t) -> p k t", k=NKH)
                        for i in range(4):
                            ks = (NKH * i) // 4
                            ke = (NKH * (i + 1)) // 4
                            dma(g, f"qgs{spar}", dstv[:, ks:ke], src[:, ks:ke])
                    strip_ld = C[f"qgs{spar}"]

                    for tt in range(4):
                        p = pA if (mb * 4 + tt) % 2 == 0 else pB

                        @blk.tensor
                        def _(t, cc=cc, mb=mb, tt=tt, p=p, spar=spar,
                              strip_ld=strip_ld, hid_ld=hid_ld):
                            t.wait_ge(S[f"qgs{spar}"], strip_ld)
                            t.wait_ge(S["qgh"], hid_ld)
                            t_guard(t, p)
                            for kh in range(NKH):
                                i2 = t.matmul(p[:],
                                              s_hidC[:, kh * TC + tt * 128:
                                                     kh * TC + (tt + 1) * 128],
                                              s_w3s[spar][:, kh * TC:kh * TC + 512],
                                              start=(kh == 0), stop=(kh == NKH - 1))
                                if kh == NKH - 1:
                                    ms(i2, "st")
                        mmd = C["st"]
                        strip_free[spar] = mmd

                        @blk.scalar
                        def _(s, mb=mb, tt=tt, p=p, mmd=mmd):
                            s.wait_ge(S["st"], mmd)
                            if last_cmb[0]:
                                s.wait_ge(S["sv"], last_cmb[0])
                            i2 = s.activation(s_g[:, tt * DM + mb * 512:
                                                  tt * DM + (mb + 1) * 512],
                                              p[:], AF.Silu)
                            ms(i2, "ss")
                        set_free(p, "ss", C["ss"])

            with ExitStack() as stwo:
                s_yc = stwo.enter_context(nc.sbuf_tensor(f"s_yc{cc}",
                                                         [128, NKY * TC], BF16))
                s_wos = [stwo.enter_context(nc.sbuf_tensor(f"s_wos{cc}_{i}",
                                                           [128, NKY * TC], BF16))
                         for i in range(2)]
                guard = {k: C[k] for k in ("st", "sv", "ss", "qo", "qcx0", "qcx1")}

                @blk.gpsimd
                def _(g, cc=cc, guard=guard):
                    for k, vv in guard.items():
                        if vv:
                            g.wait_ge(S[k], vv)
                    rk = rk_holder["rk"]
                    src = d_ycat.ap()[bass.ts(rk, DI), cc * TC:(cc + 1) * TC] \
                        .rearrange("(k p) t -> p k t", p=128)
                    dstv = s_yc[:].rearrange("p (k t) -> p k t", k=NKY)
                    for i in range(4):
                        ks = (NKY * i) // 4
                        ke = (NKY * (i + 1)) // 4
                        dma(g, "qgy", dstv[:, ks:ke], src[:, ks:ke])
                y_ld = C["qgy"]

                strip_free = [0, 0]
                for mb in range(4):
                    spar = mb % 2

                    @blk.gpsimd
                    def _(g, cc=cc, mb=mb, spar=spar):
                        if strip_free[spar]:
                            g.wait_ge(S["st"], strip_free[spar])
                        src = d_wot.ap()[:, mb * TC:(mb + 1) * TC] \
                            .rearrange("(k p) t -> p k t", p=128)
                        dstv = s_wos[spar][:].rearrange("p (k t) -> p k t", k=NKY)
                        for i in range(4):
                            ks = (NKY * i) // 4
                            ke = (NKY * (i + 1)) // 4
                            dma(g, f"qgw{spar}", dstv[:, ks:ke], src[:, ks:ke])
                    strip_ld = C[f"qgw{spar}"]

                    for tt in range(4):
                        p = pA if (mb * 4 + tt) % 2 == 0 else pB

                        @blk.tensor
                        def _(t, mb=mb, tt=tt, p=p, spar=spar,
                              strip_ld=strip_ld, y_ld=y_ld):
                            t.wait_ge(S[f"qgw{spar}"], strip_ld)
                            t.wait_ge(S["qgy"], y_ld)
                            t_guard(t, p)
                            for ky in range(NKY):
                                i2 = t.matmul(p[:],
                                              s_yc[:, ky * TC + tt * 128:
                                                   ky * TC + (tt + 1) * 128],
                                              s_wos[spar][:, ky * TC:ky * TC + 512],
                                              start=(ky == 0), stop=(ky == NKY - 1))
                                if ky == NKY - 1:
                                    ms(i2, "st")
                        mmd = C["st"]
                        strip_free[spar] = mmd

                        @blk.vector
                        def _(v, mb=mb, tt=tt, p=p, mmd=mmd):
                            v.wait_ge(S["st"], mmd)
                            v.wait_ge(S["qbo"], borep_ld)
                            i2 = v.tensor_tensor(s_s[:, tt * DM + mb * 512:
                                                     tt * DM + (mb + 1) * 512],
                                                 p[:], s_borep[:, mb * 512:
                                                               (mb + 1) * 512],
                                                 ALU.add)
                            ms(i2, "sv")
                        set_free(p, "sv", C["sv"])
            wo_pass_done_sv = C["sv"]
            g_pass_done_ss = C["ss"]

            with ExitStack() as stcm:
                s_xq = [stcm.enter_context(nc.sbuf_tensor(f"s_xq{cc}_{i}",
                                                          [128, DM], F32)) for i in range(2)]
                s_res = [stcm.enter_context(nc.sbuf_tensor(f"s_res{cc}_{i}",
                                                           [128, DM], F32)) for i in range(2)]
                for tt in range(4):
                    @blk.sync
                    def _(q, cc=cc, tt=tt, wo=wo_pass_done_sv):
                        q.wait_ge(S["sv"], wo)
                        if outd:
                            q.wait_ge(S["qo"], outd[-1])
                        dma(q, f"qcx{tt % 2}", s_xq[tt % 2][:],
                            d_xmy[cc * TC + tt * 128:cc * TC + (tt + 1) * 128, :])
                    xql = C[f"qcx{tt % 2}"]

                    @blk.vector
                    def _(v, tt=tt, xql=xql, wo=wo_pass_done_sv, gp=g_pass_done_ss):
                        v.wait_ge(S[f"qcx{tt % 2}"], xql)
                        v.wait_ge(S["ss"], gp)
                        i2 = v.tensor_tensor(s_res[tt % 2][:], s_g[:, tt * DM:(tt + 1) * DM],
                                                 s_s[:, tt * DM:(tt + 1) * DM], ALU.mult)
                        ms(i2, "sv"); v.wait_ge(S["sv"], C["sv"])
                        i2 = v.tensor_tensor(s_res[tt % 2][:], s_res[tt % 2][:],
                                             s_xq[tt % 2][:], ALU.add)
                        ms(i2, "sv")
                    cmb = C["sv"]

                    @blk.sync
                    def _(q, cc=cc, tt=tt, cmb=cmb):
                        q.wait_ge(S["sv"], cmb)
                        dma(q, "qo", d_out[cc * TC + tt * 128:cc * TC + (tt + 1) * 128, :],
                            s_res[tt % 2][:])
                    outd.append(C["qo"])
                    last_cmb[0] = cmb

        stC.close()

        @blk.sync
        def _(q):
            q.wait_ge(S["qo"], C["qo"])

    return nc


def _host_slices(inputs, r):
    f32 = np.float32
    x = np.ascontiguousarray(np.asarray(inputs["L0_attn_hidden_states"], f32)[0])
    W_in = np.asarray(inputs["L0_attn_W_in"], f32)
    conv_w = np.asarray(inputs["L0_attn_conv_w"], f32)
    conv_b = np.asarray(inputs["L0_attn_conv_b"], f32)
    W_x = np.asarray(inputs["L0_attn_W_x"], f32)
    W_dt = np.asarray(inputs["L0_attn_W_dt"], f32)
    b_dt = np.asarray(inputs["L0_attn_b_dt"], f32)
    A_log = np.asarray(inputs["L0_attn_A_log"], f32)
    Dp = np.asarray(inputs["L0_attn_D"], f32)
    W_out = np.asarray(inputs["L0_attn_W_out"], f32)
    b_out = np.asarray(inputs["L0_attn_b_out"], f32)
    w1 = np.asarray(inputs["L0_ffn_w1"], f32)
    w2 = np.asarray(inputs["L0_ffn_w2"], f32)
    w3 = np.asarray(inputs["L0_ffn_w3"], f32)

    ds = slice(r * DSL, (r + 1) * DSL)
    hs = slice(r * HSL, (r + 1) * HSL)
    per_dt = lambda a: np.ascontiguousarray(
        a[ds].reshape(4, 128, -1).transpose(1, 0, 2).reshape(128, -1))
    m = {
        "x_full": x,
        "x_myslice": np.ascontiguousarray(x[r * TSL:(r + 1) * TSL]),
        "w_in_sl": np.ascontiguousarray(np.concatenate([W_in[ds], W_in[DI:][ds]], 0)),
        "conv_w_t": per_dt(conv_w),
        "conv_b_t": per_dt(conv_b[:, None]),
        "w_x_sl": np.ascontiguousarray(W_x[:, ds]),
        "w_dt_sl": np.ascontiguousarray(W_dt[ds]),
        "b_dt_t": per_dt(b_dt[:, None]),
        "a_log_t": per_dt(A_log),
        "d_t": per_dt(Dp[:, None]),
        "w_out_sl": np.ascontiguousarray(W_out[:, ds]),
        "b_out": np.ascontiguousarray(b_out),
        "w1_sl": np.ascontiguousarray(w1[hs]),
        "w2_sl": np.ascontiguousarray(w2[hs]),
        "w3_sl": np.ascontiguousarray(w3[:, hs]),
        "ident": np.eye(128, dtype=BFD),
        "consts": np.tile(np.array([[MAGIC, 1.0, EPS_LN, 0.0]], np.float32), (128, 1)),
    }
    return m


def kernel(**inputs):
    nc = _build()
    in_maps = [_host_slices(inputs, r) for r in range(NCORE)]
    res = run_bass_kernel_spmd(nc, in_maps, core_ids=list(range(NCORE)))
    out = np.concatenate([np.asarray(res.results[r]["out_sl"]) for r in range(NCORE)],
                         axis=0)
    return out.reshape(1, L, DM).astype(np.float32, copy=False)


# version 19
# speedup vs baseline: 1.1345x; 1.1345x over previous
import numpy as np
from collections import Counter
from contextlib import ExitStack

import ml_dtypes
import concourse.bass as bass
import concourse.mybir as mybir
from concourse.bass_utils import run_bass_kernel_spmd

F32 = mybir.dt.float32
BF16 = mybir.dt.bfloat16
AF = mybir.ActivationFunctionType
ALU = mybir.AluOpType
AX = mybir.AxisListType
BFD = ml_dtypes.bfloat16

NCORE = 8
L, DM, DI, DS, DR, HF = 8192, 2048, 4096, 16, 128, 5632
DSL, HSL, TSL = DI // NCORE, HF // NCORE, L // NCORE
TA, NCA = 512, 16
TB, NCB = 512, 16
TC, NCC = 512, 2
NKH = HF // 128
NKY = DI // 128
MAGIC = 12582912.0
QCOUNT = float(HF * DM)
EPS_LN = 1e-5
W1T = [128, 128, 128, 128, 128, 64]
RG = [list(range(NCORE))]

DMA_SEMS = (["qp", "qw0", "qw1", "qc1", "qc2", "qc3",
             "qx0", "qx1", "qx2", "qx3",
             "qz0", "qz1", "qz2", "qz3",
             "qxc0", "qxc1", "qxc2", "qxc3",
             "qxr", "qbc", "qbx", "qdt0", "qdt1",
             "qgh", "qgs0", "qgs1", "qgy", "qgw0", "qgw1",
             "qcx0", "qcx1", "qo", "qbo"]
            + [f"cxs{i}" for i in range(16)]
            + [f"chs{i}" for i in range(8)]
            + [f"cys{i}" for i in range(8)])
SEM_NAMES = (["st", "sv", "ss", "sp", "cg", "cw3", "cwo"]
             + [f"cx{i}" for i in range(4)]
             + [f"ch{i}" for i in range(NCORE)]
             + [f"cy{i}" for i in range(NCORE)]
             + DMA_SEMS)


def _build():
    nc = bass.Bass("TRN2", target_bir_lowering=False)

    d_x = nc.dram_tensor("x_full", [L, DM], F32, kind="ExternalInput").ap()
    d_xmy = nc.dram_tensor("x_myslice", [TSL, DM], F32, kind="ExternalInput").ap()
    d_win = nc.dram_tensor("w_in_sl", [2 * DSL, DM], F32, kind="ExternalInput").ap()
    d_cw = nc.dram_tensor("conv_w_t", [128, 16], F32, kind="ExternalInput").ap()
    d_cb = nc.dram_tensor("conv_b_t", [128, 4], F32, kind="ExternalInput").ap()
    d_wx = nc.dram_tensor("w_x_sl", [DR + 2 * DS, DSL], F32, kind="ExternalInput").ap()
    d_wdt = nc.dram_tensor("w_dt_sl", [DSL, DR], F32, kind="ExternalInput").ap()
    d_bdt = nc.dram_tensor("b_dt_t", [128, 4], F32, kind="ExternalInput").ap()
    d_alog = nc.dram_tensor("a_log_t", [128, 64], F32, kind="ExternalInput").ap()
    d_dd = nc.dram_tensor("d_t", [128, 4], F32, kind="ExternalInput").ap()
    d_wo = nc.dram_tensor("w_out_sl", [DM, DSL], F32, kind="ExternalInput").ap()
    d_bo = nc.dram_tensor("b_out", [DM], F32, kind="ExternalInput").ap()
    d_w1 = nc.dram_tensor("w1_sl", [HSL, DM], F32, kind="ExternalInput").ap()
    d_w2 = nc.dram_tensor("w2_sl", [HSL, DM], F32, kind="ExternalInput").ap()
    d_w3 = nc.dram_tensor("w3_sl", [DM, HSL], F32, kind="ExternalInput").ap()
    d_id = nc.dram_tensor("ident", [128, 128], BF16, kind="ExternalInput").ap()
    d_const = nc.dram_tensor("consts", [128, 4], F32, kind="ExternalInput").ap()
    d_out = nc.dram_tensor("out_sl", [TSL, DM], F32, kind="ExternalOutput").ap()

    d_xdp = [nc.dram_tensor(f"xdp{g}", [DR + 2 * DS, 4 * TA], F32) for g in range(4)]
    d_xdr = [nc.dram_tensor(f"xdr{g}", [DR + 2 * DS, 4 * TA], F32) for g in range(4)]
    d_bc16 = nc.dram_tensor("bc16", [2 * DS, L], BF16)
    d_dtb = nc.dram_tensor("dtb", [128, L], BF16)
    d_xc = nc.dram_tensor("xc_f", [DSL, L], F32)
    d_sz = nc.dram_tensor("sz_f", [DSL, L], F32)
    d_hin = [nc.dram_tensor(f"hin{b}", [HSL, TSL], BF16) for b in range(NCORE)]
    d_hcat = nc.dram_tensor("hcat", [NCORE * HF, TSL], BF16)
    d_yin = [nc.dram_tensor(f"yin{b}", [DSL, TSL], BF16) for b in range(NCORE)]
    d_ycat = nc.dram_tensor("ycat", [NCORE * DI, TSL], BF16)
    d_w3qt_sl = nc.dram_tensor("w3qt_sl", [HSL, DM], BF16)
    d_w3qt = nc.dram_tensor("w3qt", [HF, DM], BF16)
    d_wot_sl = nc.dram_tensor("wot_sl", [DSL, DM], BF16)
    d_wot = nc.dram_tensor("wot", [DI, DM], BF16)
    d_gamp = nc.dram_tensor("gamp", [3, 1], F32)
    d_gamr = nc.dram_tensor("gamr", [3, 1], F32)
    d_inv = nc.dram_tensor("invq", [3, 1], F32)

    C = Counter()

    with ExitStack() as st0:
        S = {nm: st0.enter_context(nc.semaphore(nm)) for nm in SEM_NAMES}
        blk = st0.enter_context(nc.Block())

        def dma(q, qn, dst, src):
            q.dma_start(out=dst, in_=src).then_inc(S[qn], 16)
            C[qn] += 16
            return C[qn]

        def ms(instr, sn):
            instr.then_inc(S[sn], 1)
            C[sn] += 1
            return C[sn]

        def emit_barrier():
            bc = {k: C[k] for k in (["st", "sv", "ss", "sp"] + DMA_SEMS)}
            for nm in ("sync", "vector", "scalar", "tensor", "gpsimd"):
                @getattr(blk, nm)
                def _(e, bc=bc):
                    for k, v in bc.items():
                        if v:
                            e.wait_ge(S[k], v)

        sb0 = lambda nm, sh, dt=F32: st0.enter_context(nc.sbuf_tensor(nm, sh, dt))
        s_cw = sb0("s_cw", [128, 16])
        s_cb = sb0("s_cb", [128, 4])
        s_bdt = sb0("s_bdt", [128, 4])
        s_aneg = sb0("s_aneg", [128, 64])
        s_dd = sb0("s_dd", [128, 4])
        s_id = sb0("s_id", [128, 128], BF16)
        s_wdtT = sb0("s_wdtT", [128, 4 * 128], BF16)
        s_invb = sb0("s_invb", [128, 3])
        s_const = sb0("s_const", [128, 4])
        nc.const_aps.register(F32, MAGIC, s_const[:, 0:1])
        nc.const_aps.register(F32, 1.0, s_const[:, 1:2])
        nc.const_aps.register(F32, EPS_LN, s_const[:, 2:3])
        nc.const_aps.register(F32, 0.0, s_const[:, 3:4])

        pA = st0.enter_context(nc.psum_tensor("pA", [128, 512], F32))
        pB = st0.enter_context(nc.psum_tensor("pB", [128, 512], F32))
        pC2 = st0.enter_context(nc.psum_tensor("pC2", [128, 512], F32))
        pD = st0.enter_context(nc.psum_tensor("pD", [128, 512], F32))
        pW32 = st0.enter_context(nc.psum_tensor("pW32", [32, 512], F32))
        pT0 = st0.enter_context(nc.psum_tensor("pT0", [128, 512], BF16))
        pT1 = st0.enter_context(nc.psum_tensor("pT1", [128, 512], BF16))
        pG = st0.enter_context(nc.psum_tensor("pG", [3, 1], F32))
        free_after = {}

        def t_guard(t, p):
            k = p.name if hasattr(p, 'name') else str(p)
            if k in free_after:
                sn, v = free_after[k]
                t.wait_ge(S[sn], v)

        def set_free(p, sn, v):
            k = p.name if hasattr(p, 'name') else str(p)
            free_after[k] = (sn, v)

        tp_state = {"par": 0, "last": {0: 0, 1: 0}}

        def pe_transpose(src_ap, rows, cols, dst_ap, wait_sv=0, wait_ss=0):
            par = tp_state["par"] % 2
            tp_state["par"] += 1
            pT = pT0 if par == 0 else pT1

            @blk.tensor
            def _(t, pT=pT, src_ap=src_ap, rows=rows, cols=cols, par=par,
                  wait_sv=wait_sv, wait_ss=wait_ss):
                if wait_sv:
                    t.wait_ge(S["sv"], wait_sv)
                if wait_ss:
                    t.wait_ge(S["ss"], wait_ss)
                if tp_state["last"][par]:
                    t.wait_ge(S["sv"], tp_state["last"][par])
                i = t.transpose(pT[0:cols, 0:rows], src_ap, s_id[0:rows, 0:rows])
                ms(i, "st")
            tdone = C["st"]

            @blk.vector
            def _(v, pT=pT, dst_ap=dst_ap, rows=rows, cols=cols, tdone=tdone,
                  par=par):
                v.wait_ge(S["st"], tdone)
                i = v.tensor_copy(dst_ap, pT[0:cols, 0:rows])
                tp_state["last"][par] = ms(i, "sv")
            return C["sv"]

        @blk.sync
        def _(q):
            for dst, src in [(s_cw, d_cw), (s_cb, d_cb), (s_bdt, d_bdt),
                             (s_dd, d_dd), (s_id, d_id), (s_const, d_const)]:
                dma(q, "qp", dst[:], src)
        ld_params = C["qp"]

        with ExitStack() as stQ:
            sbQ = lambda nm, sh, dt=F32: stQ.enter_context(nc.sbuf_tensor(nm, sh, dt))
            s_alog = sbQ("s_alog", [128, 64])
            s_stg = [sbQ(f"s_stg{i}", [128, DM]) for i in range(2)]
            s_scrQ = sbQ("s_scrQ", [128, DM])
            s_abs = sbQ("s_abs", [128, 28])
            s_one = sbQ("s_one", [128, 1])
            s_g3 = sbQ("s_g3", [3, 1])

            @blk.sync
            def _(q):
                dma(q, "qp", s_alog[:], d_alog)
            ld_alog = C["qp"]

            @blk.scalar
            def _(s):
                s.wait_ge(S["qp"], ld_alog)
                i = s.activation(s_aneg[:], s_alog[:], AF.Exp)
                ms(i, "ss")
            aexp = C["ss"]

            @blk.vector
            def _(v):
                v.wait_ge(S["ss"], aexp)
                v.tensor_scalar(s_aneg[:], s_aneg[:], -1.0, None, ALU.mult)
                v.memset(s_abs[:], 0.0)
                i = v.memset(s_one[:], 1.0)
                ms(i, "sv")
            absz = C["sv"]

            stg_gate = [None, None]
            tiles = []
            r0 = 0
            for i, rr in enumerate(W1T):
                tiles.append((d_w1[r0:r0 + rr, :], rr, DM, i))
                r0 += rr
            r0 = 0
            for i, rr in enumerate(W1T):
                tiles.append((d_w2[r0:r0 + rr, :], rr, DM, 6 + i))
                r0 += rr
            for i in range(16):
                tiles.append((d_w3[i * 128:(i + 1) * 128, :], 128, HSL, 12 + i))

            for ti, (src, rr, cc, col) in enumerate(tiles):
                par = ti % 2
                stg = s_stg[par]

                @blk.sync
                def _(q, par=par, stg=stg, src=src, rr=rr, cc=cc):
                    if stg_gate[par]:
                        q.wait_ge(S[stg_gate[par][0]], stg_gate[par][1])
                    dma(q, f"qw{par}", stg[0:rr, 0:cc], src)
                ld = C[f"qw{par}"]

                @blk.scalar
                def _(s, stg=stg, rr=rr, cc=cc, col=col, ld=ld, ti=ti, par=par):
                    s.wait_ge(S[f"qw{par}"], ld)
                    if ti == 0:
                        s.wait_ge(S["sv"], absz)
                    else:
                        s.wait_ge(S["ss"], C["ss"])
                    i = s.activation(s_scrQ[0:rr, 0:cc], stg[0:rr, 0:cc], AF.Abs,
                                     accum_out=s_abs[0:rr, col:col + 1])
                    stg_gate[par] = ("ss", ms(i, "ss"))
            all_abs = C["ss"]

            @blk.vector
            def _(v):
                v.wait_ge(S["ss"], all_abs)
                for k in range(3):
                    cols = list(range(k * 6, k * 6 + 6)) if k < 2 else list(range(12, 28))
                    i = v.tensor_tensor(s_scrQ[:, k:k + 1], s_abs[:, cols[0]:cols[0] + 1],
                                        s_abs[:, cols[1]:cols[1] + 1], ALU.add)
                    for cx in cols[2:]:
                        ms(i, "sv"); v.wait_ge(S["sv"], C["sv"])
                        i = v.tensor_tensor(s_scrQ[:, k:k + 1], s_scrQ[:, k:k + 1],
                                            s_abs[:, cx:cx + 1], ALU.add)
                ms(i, "sv"); v.wait_ge(S["sv"], C["sv"])
                i = v.tensor_copy(s_scrQ[:, 4:7], s_scrQ[:, 0:3])
                ms(i, "sv")
            sums_done = C["sv"]

            @blk.tensor
            def _(t):
                t.wait_ge(S["sv"], sums_done)
                i = t.matmul(pG[:], s_scrQ[:, 4:7], s_one[:], start=True, stop=True)
                ms(i, "st")
            gmm = C["st"]

            @blk.scalar
            def _(s):
                s.wait_ge(S["st"], gmm)
                i = s.copy(s_g3[:], pG[:])
                ms(i, "ss")
            set_free(pG, "ss", C["ss"])
            gev = C["ss"]

            @blk.sync
            def _(q):
                q.wait_ge(S["ss"], gev)
                dma(q, "qc1", d_gamp[:, :], s_g3[:])
            gst = C["qc1"]

            @blk.gpsimd
            def _(g):
                g.wait_ge(S["qc1"], gst)
                g.collective_compute("AllReduce", ALU.add, replica_groups=RG,
                                     ins=[d_gamp.ap().opt()],
                                     outs=[d_gamr.ap().opt()]).then_inc(S["cg"], 1)

            @blk.sync
            def _(q):
                q.wait_ge(S["cg"], 1)
                dma(q, "qc1", s_g3[:], d_gamr[:, :])
            grd = C["qc1"]

            @blk.vector
            def _(v):
                v.wait_ge(S["qc1"], grd)
                i = v.tensor_scalar(s_g3[:], s_g3[:], 1.0 / QCOUNT, 1e-8, ALU.mult, ALU.add)
                ms(i, "sv"); v.wait_ge(S["sv"], C["sv"])
                i = v.reciprocal(s_g3[:], s_g3[:])
                ms(i, "sv")
            invd = C["sv"]

            @blk.sync
            def _(q):
                q.wait_ge(S["sv"], invd)
                dma(q, "qc1", d_inv[:, :], s_g3[:])
                q.wait_ge(S["qc1"], C["qc1"])
                src = d_inv.rearrange("n o -> o n").broadcast_to([128, 3])
                dma(q, "qc1", s_invb[:], src)
            invb_ld = C["qc1"]

        stA = ExitStack()
        sbA = lambda nm, sh, dt=F32: stA.enter_context(nc.sbuf_tensor(nm, sh, dt))
        s_winT = sbA("s_winT", [128, 16 * 8 * 128], BF16)
        s_wxT = sbA("s_wxT", [128, 4 * 160], BF16)
        s_w1qT = sbA("s_w1qT", [128, 16 * HSL], BF16)
        s_w2qT = sbA("s_w2qT", [128, 16 * HSL], BF16)

        with ExitStack() as stP:
            sbP = lambda nm, sh, dt=F32: stP.enter_context(nc.sbuf_tensor(nm, sh, dt))
            s_stg = [sbP(f"s_stgp{i}", [128, DM]) for i in range(2)]
            s_scrQ = sbP("s_scrQp", [128, DM])
            s_qbf = sbP("s_qbf", [128, DM], BF16)
            s_as3 = [sbP(f"s_as3{i}", [128, DM], BF16) for i in range(6)]
            s_aso = [sbP(f"s_aso{i}", [128, DM], BF16) for i in range(4)]

            last_qv = [0]
            last_tp = [0]

            def quant_tiles(dsrc, rows_list, ncols, kidx, emit_tp):
                r0 = 0
                for i, rr in enumerate(rows_list):
                    par = i % 2
                    stg = s_stg[par]

                    @blk.sync
                    def _(q, par=par, stg=stg, r0=r0, rr=rr, dsrc=dsrc, ncols=ncols):
                        if stg_gate[par]:
                            q.wait_ge(S[stg_gate[par][0]], stg_gate[par][1])
                        q.wait_ge(S["qc1"], invb_ld)
                        dma(q, f"qw{par}", stg[0:rr, 0:ncols], dsrc[r0:r0 + rr, :])
                    ld = C[f"qw{par}"]

                    @blk.scalar
                    def _(s, stg=stg, rr=rr, ncols=ncols, kidx=kidx, ld=ld, par=par):
                        s.wait_ge(S[f"qw{par}"], ld)
                        if last_qv[0]:
                            s.wait_ge(S["sv"], last_qv[0])
                        i2 = s.activation(s_scrQ[0:rr, 0:ncols], stg[0:rr, 0:ncols],
                                          AF.Identity, scale=s_invb[0:rr, kidx:kidx + 1],
                                          bias=MAGIC)
                        stg_gate[par] = ("ss", ms(i2, "ss"))
                    q1 = C["ss"]

                    @blk.vector
                    def _(v, rr=rr, ncols=ncols, q1=q1, i=i):
                        v.wait_ge(S["ss"], q1)
                        if last_tp[0]:
                            v.wait_ge(S["st"], last_tp[0])
                        i2 = v.tensor_scalar(s_scrQ[0:rr, 0:ncols], s_scrQ[0:rr, 0:ncols],
                                             MAGIC, 1.0, ALU.subtract, ALU.min)
                        ms(i2, "sv"); v.wait_ge(S["sv"], C["sv"])
                        i2 = v.tensor_scalar(s_qbf[0:rr, 0:ncols], s_scrQ[0:rr, 0:ncols],
                                             -1.0, None, ALU.max)
                        last_qv[0] = ms(i2, "sv")
                    emit_tp(i, rr, C["sv"])
                    last_tp[0] = C["st"]
                    r0 += rr

            for kidx, dstT in ((0, s_w1qT), (1, s_w2qT)):
                def tp(i, rr, qv, dstT=dstT):
                    for ki in range(16):
                        pe_transpose(s_qbf[0:rr, ki * 128:(ki + 1) * 128], rr, 128,
                                     dstT[:, ki * HSL + i * 128:ki * HSL + i * 128 + rr],
                                     wait_sv=qv)
                quant_tiles(d_w1 if kidx == 0 else d_w2, W1T, DM, kidx, tp)

            def tp3(i, rr, qv):
                for hb in range(6):
                    cols = W1T[hb]
                    pe_transpose(s_qbf[0:128, hb * 128:hb * 128 + cols], 128, cols,
                                 s_as3[hb][0:cols, i * 128:(i + 1) * 128], wait_sv=qv)
            quant_tiles(d_w3, [128] * 16, HSL, 2, tp3)
            w3t = C["sv"]

            @blk.sync
            def _(q):
                q.wait_ge(S["sv"], w3t)
                r0 = 0
                for hb in range(6):
                    rr = W1T[hb]
                    dma(q, "qc2", d_w3qt_sl[r0:r0 + rr, :], s_as3[hb][0:rr, :])
                    r0 += rr
            w3st = C["qc2"]

            @blk.gpsimd
            def _(g):
                g.wait_ge(S["qc2"], w3st)
                g.collective_compute("AllGather", ALU.bypass, replica_groups=RG,
                                     ins=[d_w3qt_sl.ap().opt()],
                                     outs=[d_w3qt.ap().opt()]).then_inc(S["cw3"], 1)

            for ki in range(16):
                par = ki % 2
                stg = s_stg[par]

                @blk.sync
                def _(q, par=par, stg=stg, ki=ki):
                    if stg_gate[par]:
                        q.wait_ge(S[stg_gate[par][0]], stg_gate[par][1])
                    dma(q, f"qw{par}", stg[:, 0:DSL], d_wo[ki * 128:(ki + 1) * 128, :])
                ld = C[f"qw{par}"]

                @blk.vector
                def _(v, stg=stg, ld=ld, ki=ki, par=par, tp=C["st"]):
                    v.wait_ge(S[f"qw{par}"], ld)
                    if tp:
                        v.wait_ge(S["st"], tp)
                    i2 = v.tensor_copy(s_qbf[:, 0:DSL], stg[:, 0:DSL])
                    stg_gate[par] = ("sv", ms(i2, "sv"))
                qv = C["sv"]
                for db in range(4):
                    pe_transpose(s_qbf[:, db * 128:(db + 1) * 128], 128, 128,
                                 s_aso[db][:, ki * 128:(ki + 1) * 128], wait_sv=qv)
            wot = C["sv"]

            @blk.sync
            def _(q):
                q.wait_ge(S["sv"], wot)
                for db in range(4):
                    dma(q, "qc3", d_wot_sl[db * 128:(db + 1) * 128, :], s_aso[db][:])
            wost = C["qc3"]

            @blk.gpsimd
            def _(g):
                g.wait_ge(S["qc3"], wost)
                g.collective_compute("AllGather", ALU.bypass, replica_groups=RG,
                                     ins=[d_wot_sl.ap().opt()],
                                     outs=[d_wot.ap().opt()]).then_inc(S["cwo"], 1)

            for eo in range(8):
                par = eo % 2
                stg = s_stg[par]

                @blk.sync
                def _(q, par=par, stg=stg, eo=eo):
                    if stg_gate[par]:
                        q.wait_ge(S[stg_gate[par][0]], stg_gate[par][1])
                    dma(q, f"qw{par}", stg[:], d_win[eo * 128:(eo + 1) * 128, :])
                ld = C[f"qw{par}"]

                @blk.vector
                def _(v, stg=stg, ld=ld, eo=eo, par=par, tp=C["st"]):
                    v.wait_ge(S[f"qw{par}"], ld)
                    if tp:
                        v.wait_ge(S["st"], tp)
                    i2 = v.tensor_copy(s_qbf[:], stg[:])
                    stg_gate[par] = ("sv", ms(i2, "sv"))
                qv = C["sv"]
                for ki in range(16):
                    pe_transpose(s_qbf[:, ki * 128:(ki + 1) * 128], 128, 128,
                                 s_winT[:, (ki * 8 + eo) * 128:(ki * 8 + eo) * 128 + 128],
                                 wait_sv=qv)

            @blk.sync
            def _(q):
                for par in range(2):
                    if stg_gate[par]:
                        q.wait_ge(S[stg_gate[par][0]], stg_gate[par][1])
                dma(q, "qw0", s_stg[0][0:128, 0:DSL], d_wx[0:128, :])
                dma(q, "qw1", s_stg[1][0:32, 0:DSL], d_wx[128:160, :])
            ldx0, ldx1 = C["qw0"], C["qw1"]

            @blk.vector
            def _(v, tp=C["st"]):
                v.wait_ge(S["qw0"], ldx0)
                v.wait_ge(S["qw1"], ldx1)
                if tp:
                    v.wait_ge(S["st"], tp)
                v.tensor_copy(s_qbf[0:128, 0:DSL], s_stg[0][0:128, 0:DSL])
                i2 = v.tensor_copy(s_qbf[0:32, DSL:2 * DSL], s_stg[1][0:32, 0:DSL])
                stg_gate[0] = stg_gate[1] = ("sv", ms(i2, "sv"))
            qv = C["sv"]
            for ki in range(4):
                pe_transpose(s_qbf[0:128, ki * 128:(ki + 1) * 128], 128, 128,
                             s_wxT[:, ki * 160:ki * 160 + 128], wait_sv=qv)
                pe_transpose(s_qbf[0:32, DSL + ki * 128:DSL + (ki + 1) * 128], 32, 128,
                             s_wxT[:, ki * 160 + 128:ki * 160 + 160], wait_sv=qv)

            @blk.sync
            def _(q):
                if stg_gate[0]:
                    q.wait_ge(S[stg_gate[0][0]], stg_gate[0][1])
                dma(q, "qw0", s_stg[0][0:128, 0:512].rearrange("p (d r) -> p d r", d=4),
                    d_wdt.rearrange("(d p) r -> p d r", p=128))
            lddt = C["qw0"]

            @blk.vector
            def _(v, tp=C["st"]):
                v.wait_ge(S["qw0"], lddt)
                if tp:
                    v.wait_ge(S["st"], tp)
                i2 = v.tensor_copy(s_qbf[0:128, 0:512], s_stg[0][0:128, 0:512])
                ms(i2, "sv")
            qv2 = C["sv"]
            for dt in range(4):
                pe_transpose(s_qbf[0:128, dt * 128:(dt + 1) * 128], 128, 128,
                             s_wdtT[:, dt * 128:(dt + 1) * 128], wait_sv=qv2)

        set_free(pT0, "sv", tp_state["last"][0])
        set_free(pT1, "sv", tp_state["last"][1])
        emit_barrier()

        s_xp = [sbA(f"s_xp{i}", [128, DM]) for i in range(5)]
        s_xnb = [sbA(f"s_xnb{i}", [128, DM], BF16) for i in range(2)]
        s_scrA = sbA("s_scrA", [128, DM])
        s_stats = sbA("s_stats", [128, 16])
        s_xnT = [sbA(f"s_xnT{i}", [128, 16 * TA], BF16) for i in range(2)]
        s_xh = sbA("s_xh", [128, 4 * (TA + 3)])
        s_xc = sbA("s_xc", [128, 4 * TA])
        s_xcb = sbA("s_xcb", [128, 4 * TA], BF16)
        s_szb = sbA("s_szb", [128, 4 * TA])
        s_xd0 = sbA("s_xd0", [128, TA])
        s_xd1 = sbA("s_xd1", [32, TA])
        s_sig = sbA("s_sig", [128, TA])
        s_hbf = [sbA(f"s_hbf{i}", [128, TA], BF16) for i in range(2)]

        @blk.vector
        def _(v):
            i = v.memset(s_xh[:], 0.0)
            ms(i, "sv")
        halo0 = C["sv"]

        xload = {}
        norm_done = {}
        tpgrp_done = {}
        stats_gate = 0
        ffn_last_par = [0, 0]
        evx_done = {}
        tail_done = {}
        xcdma = {}
        zdma = {}
        xcb_free = 0
        zmm = {}
        zev_c = [0, 0, 0, 0]
        last_h = [None]
        hid_prev = [0]

        nmr_of = {}

        def emit_stats(c):
            nonlocal stats_gate
            for pt in range(4):
                @blk.sync
                def _(q, c=c, pt=pt):
                    if c * 4 + pt >= 5:
                        q.wait_ge(S["sv"], norm_done[c * 4 + pt - 5])
                    elif c * 4 + pt == 4:
                        q.wait_ge(S["sv"], norm_done[0])
                    dma(q, f"qx{pt}", s_xp[(c * 4 + pt) % 5][:],
                        d_x[c * TA + pt * 128:c * TA + (pt + 1) * 128, :])
                xload[(c, pt)] = C[f"qx{pt}"]

            @blk.scalar
            def _(s, c=c):
                if stats_gate:
                    s.wait_ge(S["sv"], stats_gate)
                for pt in range(4):
                    s.wait_ge(S[f"qx{pt}"], xload[(c, pt)])
                    if pt > 0 or c > 0:
                        s.wait_ge(S["ss"], C["ss"])
                    i = s.activation(s_scrA[:], s_xp[(c * 4 + pt) % 5][:], AF.Square,
                                     accum_out=s_stats[:, pt:pt + 1])
                    ms(i, "ss")
            stats_done = C["ss"]

            @blk.vector
            def _(v, c=c):
                if stats_gate:
                    v.wait_ge(S["sv"], stats_gate)
                for pt in range(4):
                    v.wait_ge(S[f"qx{pt}"], xload[(c, pt)])
                    i = v.tensor_reduce(s_stats[:, 4 + pt:5 + pt], s_xp[(c * 4 + pt) % 5][:],
                                        AX.X, ALU.add)
                    ms(i, "sv")
            sums_done_p = C["sv"]

            @blk.vector
            def _(v, stats_done=stats_done, sums_done_p=sums_done_p):
                v.wait_ge(S["ss"], stats_done)
                v.wait_ge(S["sv"], sums_done_p)
                i = v.tensor_scalar(s_stats[:, 4:8], s_stats[:, 4:8], -1.0 / DM, None,
                                    ALU.mult)
                ms(i, "sv"); v.wait_ge(S["sv"], C["sv"])
                i = v.tensor_scalar(s_stats[:, 0:4], s_stats[:, 0:4], 1.0 / DM, None,
                                    ALU.mult)
                ms(i, "sv"); v.wait_ge(S["sv"], C["sv"])
                i = v.tensor_tensor(s_stats[:, 8:12], s_stats[:, 4:8], s_stats[:, 4:8],
                                    ALU.mult)
                ms(i, "sv"); v.wait_ge(S["sv"], C["sv"])
                i = v.tensor_tensor(s_stats[:, 0:4], s_stats[:, 0:4],
                                    s_stats[:, 8:12], ALU.subtract)
                ms(i, "sv")
            var_done = C["sv"]

            @blk.scalar
            def _(s, var_done=var_done):
                s.wait_ge(S["sv"], var_done)
                i = s.activation(s_stats[:, 8:12], s_stats[:, 0:4], AF.Ln, bias=EPS_LN)
                ms(i, "ss"); s.wait_ge(S["ss"], C["ss"])
                i = s.activation(s_stats[:, 8:12], s_stats[:, 8:12], AF.Exp, scale=-0.5)
                ms(i, "ss")
            rstd_done = C["ss"]

            @blk.vector
            def _(v, rstd_done=rstd_done):
                v.wait_ge(S["ss"], rstd_done)
                i = v.tensor_tensor(s_stats[:, 12:16], s_stats[:, 4:8],
                                    s_stats[:, 8:12], ALU.mult)
                ms(i, "sv")
            nmr_of[c] = C["sv"]
            stats_gate = C["sv"]

        emit_stats(0)
        for c in range(NCA):
            nmr_done = nmr_of[c]
            for pt in range(4):
                gi = c * 4 + pt

                @blk.vector
                def _(v, pt=pt, gi=gi, nmr_done=nmr_done):
                    v.wait_ge(S["sv"], nmr_done)
                    if gi >= 2 and (gi - 2) in tpgrp_done:
                        v.wait_ge(S["st"], tpgrp_done[gi - 2])
                    nmr_b = s_stats[:, 12 + pt:13 + pt].broadcast_to([128, DM])
                    i = v.scalar_tensor_tensor(s_xnb[pt % 2][:], s_xp[(c * 4 + pt) % 5][:],
                                               s_stats[:, 8 + pt:9 + pt], nmr_b,
                                               ALU.mult, ALU.add)
                    norm_done[gi] = ms(i, "sv")

                for g4 in range(4):
                    par = tp_state["par"] % 2
                    tp_state["par"] += 1
                    pT = pT0 if par == 0 else pT1

                    @blk.tensor
                    def _(t, pT=pT, pt=pt, g4=g4, gi=gi, par=par):
                        t.wait_ge(S["sv"], norm_done[gi])
                        if tp_state["last"][par]:
                            t.wait_ge(S["sv"], tp_state["last"][par])
                        for j in range(4):
                            ki = g4 * 4 + j
                            i2 = t.transpose(pT[0:128, j * 128:(j + 1) * 128],
                                             s_xnb[pt % 2][:, ki * 128:(ki + 1) * 128],
                                             s_id[:])
                            if j == 3:
                                ms(i2, "st")
                    tpgrp_done[gi] = C["st"]
                    tdone = C["st"]

                    @blk.vector
                    def _(v, pT=pT, pt=pt, g4=g4, tdone=tdone, par=par, c=c,
                          flm=None):
                        v.wait_ge(S["st"], tdone)
                        if ffn_last_par[c % 2]:
                            v.wait_ge(S["st"], ffn_last_par[c % 2])
                        dst = s_xnT[c % 2][:].rearrange("p (k t) -> p k t", k=16)[
                            :, g4 * 4:(g4 + 1) * 4, pt * 128:(pt + 1) * 128]
                        i2 = v.tensor_copy(dst, pT[:, 0:512].rearrange("p (k t) -> p k t", k=4))
                        tp_state["last"][par] = ms(i2, "sv")
            xnT_done = C["sv"]

            if c + 1 < NCA:
                emit_stats(c + 1)

            PSROT = [pA, pB, pC2, pD]
            for eo in range(8):
                p = PSROT[eo % 4]

                @blk.tensor
                def _(t, c=c, eo=eo, p=p, xnT_done=xnT_done):
                    t.wait_ge(S["sv"], xnT_done)
                    t_guard(t, p)
                    for ki in range(16):
                        i2 = t.matmul(p[:], s_winT[:, (ki * 8 + eo) * 128:
                                                   (ki * 8 + eo + 1) * 128],
                                      s_xnT[c % 2][:, ki * TA:ki * TA + 512],
                                      start=(ki == 0), stop=(ki == 15))
                        if ki == 15:
                            ms(i2, "st")
                mmdone = C["st"]

                if eo < 4:
                    dt = eo

                    @blk.vector
                    def _(v, c=c, dt=dt, p=p, mmdone=mmdone):
                        v.wait_ge(S["st"], mmdone)
                        if c == 0:
                            v.wait_ge(S["sv"], halo0)
                        i2 = v.tensor_copy(s_xh[:, dt * (TA + 3) + 3:
                                               dt * (TA + 3) + 3 + TA], p[:])
                        evx_done[(c, dt)] = ms(i2, "sv")
                    set_free(p, "sv", evx_done[(c, dt)])
                else:
                    dt = eo - 4
                    zmm[dt] = (mmdone, p)

            @blk.scalar
            def _(s, c=c, zmm=dict(zmm)):
                for dt in range(4):
                    mmdone, p = zmm[dt]
                    s.wait_ge(S["st"], mmdone)
                    if (c - 1, dt) in zdma:
                        s.wait_ge(S[f"qz{dt}"], zdma[(c - 1, dt)])
                    i2 = s.activation(s_szb[:, dt * TA:(dt + 1) * TA], p[:], AF.Silu)
                    ms(i2, "ss")
                    zev_c[dt] = C["ss"]
                    set_free(p, "ss", C["ss"])

            for dt in range(4):
                @blk.sync
                def _(q, c=c, dt=dt, zev=zev_c[dt]):
                    q.wait_ge(S["ss"], zev)
                    dma(q, f"qz{dt}", d_sz[dt * 128:(dt + 1) * 128, c * TA:(c + 1) * TA],
                        s_szb[:, dt * TA:(dt + 1) * TA])
                zdma[(c, dt)] = C[f"qz{dt}"]

            for dt in range(4):
                @blk.vector
                def _(v, c=c, dt=dt):
                    v.wait_ge(S["sv"], evx_done[(c, dt)])
                    if (c - 1, dt) in xcdma:
                        v.wait_ge(S[f"qxc{dt}"], xcdma[(c - 1, dt)])
                    base = dt * (TA + 3)
                    xs = s_xc[:, dt * TA:(dt + 1) * TA]
                    i2 = v.tensor_scalar(xs, s_xh[:, base:base + TA],
                                         s_cw[:, dt * 4:dt * 4 + 1], None, ALU.mult)
                    for k in range(1, 4):
                        ms(i2, "sv"); v.wait_ge(S["sv"], C["sv"])
                        i2 = v.scalar_tensor_tensor(xs, s_xh[:, base + k:base + k + TA],
                                                    s_cw[:, dt * 4 + k:dt * 4 + k + 1],
                                                    xs, ALU.mult, ALU.add)
                    ms(i2, "sv"); v.wait_ge(S["sv"], C["sv"])
                    i2 = v.tensor_copy(s_xh[:, base:base + 3],
                                       s_xh[:, base + TA:base + TA + 3])
                    tail_done[(c, dt)] = ms(i2, "sv")

                @blk.scalar
                def _(s, c=c, dt=dt):
                    s.wait_ge(S["sv"], tail_done[(c, dt)])
                    xs = s_xc[:, dt * TA:(dt + 1) * TA]
                    i2 = s.activation(xs, xs, AF.Silu, bias=s_cb[:, dt:dt + 1])
                    ms(i2, "ss")
                silu_done = C["ss"]

                @blk.vector
                def _(v, c=c, dt=dt, silu_done=silu_done):
                    v.wait_ge(S["ss"], silu_done)
                    if xcb_free:
                        v.wait_ge(S["st"], xcb_free)
                    i2 = v.tensor_copy(s_xcb[:, dt * TA:(dt + 1) * TA],
                                       s_xc[:, dt * TA:(dt + 1) * TA])
                    ms(i2, "sv")
                xcbf = C["sv"]

                @blk.sync
                def _(q, c=c, dt=dt, sd=silu_done):
                    q.wait_ge(S["ss"], sd)
                    dma(q, f"qxc{dt}", d_xc[dt * 128:(dt + 1) * 128, c * TA:(c + 1) * TA],
                        s_xc[:, dt * TA:(dt + 1) * TA])
                xcdma[(c, dt)] = C[f"qxc{dt}"]
            xcb_done = C["sv"]

            for ho in range(6):
                rr = W1T[ho]

                @blk.tensor
                def _(t, c=c, ho=ho, rr=rr, xnT_done=xnT_done):
                    t_guard(t, pA)
                    for ki in range(16):
                        i2 = t.matmul(pA[0:rr, :],
                                      s_w1qT[:, ki * HSL + ho * 128:
                                             ki * HSL + ho * 128 + rr],
                                      s_xnT[c % 2][:, ki * TA:ki * TA + 512],
                                      start=(ki == 0), stop=(ki == 15))
                        if ki == 15:
                            ms(i2, "st")
                x1mm = C["st"]

                @blk.tensor
                def _(t, c=c, ho=ho, rr=rr):
                    t_guard(t, pB)
                    for ki in range(16):
                        i2 = t.matmul(pB[0:rr, :],
                                      s_w2qT[:, ki * HSL + ho * 128:
                                             ki * HSL + ho * 128 + rr],
                                      s_xnT[c % 2][:, ki * TA:ki * TA + 512],
                                      start=(ki == 0), stop=(ki == 15))
                        if ki == 15:
                            ms(i2, "st")
                x2mm = C["st"]
                if ho == 5:
                    ffn_last_par[c % 2] = x2mm

                @blk.scalar
                def _(s, rr=rr, x1mm=x1mm):
                    s.wait_ge(S["st"], x1mm)
                    if hid_prev[0]:
                        s.wait_ge(S["sv"], hid_prev[0])
                    i2 = s.activation(s_sig[0:rr, :], pA[0:rr, :], AF.Sigmoid)
                    ms(i2, "ss")
                sgd = C["ss"]
                set_free(pA, "ss", sgd)

                @blk.vector
                def _(v, c=c, ho=ho, rr=rr, sgd=sgd, x2mm=x2mm):
                    v.wait_ge(S["ss"], sgd)
                    v.wait_ge(S["st"], x2mm)
                    if last_h[0] is not None:
                        v.wait_ge(S[last_h[0][0]], last_h[0][1])
                    i2 = v.tensor_tensor(s_hbf[ho % 2][0:rr, :], s_sig[0:rr, :],
                                         pB[0:rr, :], ALU.mult)
                    hid_prev[0] = ms(i2, "sv")
                hid = C["sv"]
                set_free(pB, "sv", hid)

                @blk.sync
                def _(q, c=c, ho=ho, rr=rr, hid=hid):
                    q.wait_ge(S["sv"], hid)
                    dma(q, f"chs{c // 2}", d_hin[c // 2][ho * 128:ho * 128 + rr,
                                               (c % 2) * TA:(c % 2 + 1) * TA],
                        s_hbf[ho % 2][0:rr, :])
                last_h[0] = (f"chs{c // 2}", C[f"chs{c // 2}"])

            @blk.tensor
            def _(t, c=c, xcb_done=xcb_done):
                t.wait_ge(S["sv"], xcb_done)
                t_guard(t, pA)
                for ki in range(4):
                    i2 = t.matmul(pA[:], s_wxT[:, ki * 160:ki * 160 + 128],
                                  s_xcb[:, ki * TA:ki * TA + 512],
                                  start=(ki == 0), stop=(ki == 3))
                    if ki == 3:
                        ms(i2, "st")
                t_guard(t, pW32)
                for ki in range(4):
                    i2 = t.matmul(pW32[:], s_wxT[:, ki * 160 + 128:ki * 160 + 160],
                                  s_xcb[:, ki * TA:ki * TA + 512],
                                  start=(ki == 0), stop=(ki == 3))
                    if ki == 3:
                        ms(i2, "st")
            wx_done = C["st"]
            xcb_free = wx_done

            @blk.scalar
            def _(s, c=c, wx_done=wx_done):
                s.wait_ge(S["st"], wx_done)
                if c > 0:
                    s.wait_ge(S[f"cxs{c - 1}"], 32)
                s.copy(s_xd0[:], pA[:])
                i2 = s.copy(s_xd1[:], pW32[:])
                ms(i2, "ss")
            wx_ev = C["ss"]
            set_free(pA, "ss", wx_ev)
            set_free(pW32, "ss", wx_ev)

            @blk.sync
            def _(q, c=c, wx_ev=wx_ev):
                q.wait_ge(S["ss"], wx_ev)
                co = (c % 4) * TA
                dma(q, f"cxs{c}", d_xdp[c // 4][0:128, co:co + TA], s_xd0[:])
                dma(q, f"cxs{c}", d_xdp[c // 4][128:160, co:co + TA], s_xd1[:])

            if c % 4 == 3:
                @blk.gpsimd
                def _(g, c=c):
                    for i in range(4):
                        g.wait_ge(S[f"cxs{c - 3 + i}"], 32)
                    gi = c // 4
                    g.collective_compute("AllReduce", ALU.add, replica_groups=RG,
                                         ins=[d_xdp[gi].ap().opt()],
                                         outs=[d_xdr[gi].ap().opt()]
                                         ).then_inc(S[f"cx{gi}"], 1)

            if c % 2 == 1:
                b = c // 2

                @blk.gpsimd
                def _(g, b=b, w=C[f"chs{c // 2}"]):
                    g.wait_ge(S[f"chs{b}"], w)
                    g.collective_compute(
                        "AllGather", ALU.bypass, replica_groups=RG,
                        ins=[d_hin[b].ap().opt()],
                        outs=[d_hcat.ap()[b * HF:(b + 1) * HF, :].opt()]
                    ).then_inc(S[f"ch{b}"], 1)

        stA.close()
        emit_barrier()

        stB = ExitStack()
        sbB = lambda nm, sh, dt=F32: stB.enter_context(nc.sbuf_tensor(nm, sh, dt))
        s_dA = [sbB(f"s_dA{i}", [128, 16 * TB]) for i in range(2)]
        s_bb = [sbB(f"s_bb{i}", [128, 16 * TB]) for i in range(2)]
        s_dtc = [sbB(f"s_dtc{i}", [128, TB], BF16) for i in range(2)]
        s_brep = sbB("s_brep", [128, 16 * TB], BF16)
        s_crep = sbB("s_crep", [128, 16 * TB], BF16)
        s_delta = sbB("s_delta", [128, 4 * TB])
        s_xcB = [sbB(f"s_xcB{i}", [128, TB]) for i in range(5)]
        s_szB = [sbB(f"s_szB{i}", [128, TB]) for i in range(5)]
        s_dU = sbB("s_dU", [128, 4 * TB])
        s_ys = sbB("s_ys", [128, TB])
        s_ybf = [sbB(f"s_ybf{i}", [128, TB], BF16) for i in range(2)]
        s_car = sbB("s_car", [128, 64])
        s_t16 = sbB("s_t16", [128, 16])
        s_ones = sbB("s_ones", [1, 128], BF16)
        s_bcrow = sbB("s_bcrow", [1, 2 * TB], BF16)

        @blk.gpsimd
        def _(g):
            for gi in range(4):
                g.wait_ge(S[f"cx{gi}"], 1)
            for gi in range(4):
                dma(g, "qxr", s_dA[0][:, gi * 4 * TB:(gi + 1) * 4 * TB],
                    d_xdr[gi][0:128, :])
                dma(g, "qxr", s_bb[0][0:32, gi * 4 * TB:(gi + 1) * 4 * TB],
                    d_xdr[gi][128:160, :])
        xdr_ld = C["qxr"]

        @blk.vector
        def _(v):
            v.wait_ge(S["qxr"], xdr_ld)
            v.memset(s_car[:], 0.0)
            i = v.tensor_copy(s_brep[:], s_dA[0][:])
            ms(i, "sv")
        prol_dt = C["sv"]

        @blk.scalar
        def _(s):
            s.wait_ge(S["sv"], prol_dt)
            dma(s, "qbc", d_dtb[:, :], s_brep[:])
        dtb_st = C["qbc"]

        @blk.vector
        def _(v):
            v.wait_ge(S["qbc"], dtb_st)
            v.wait_ge(S["sv"], C["sv"])
            i = v.tensor_copy(s_brep[0:32, 0:L], s_bb[0][0:32, :])
            ms(i, "sv")
        prol = C["sv"]

        @blk.scalar
        def _(s):
            s.wait_ge(S["sv"], prol)
            dma(s, "qbc", d_bc16[:, :], s_brep[0:32, 0:L])
        bc_st = C["qbc"]

        @blk.vector
        def _(v):
            i = v.memset(s_ones[:], 1.0)
            ms(i, "sv")
        ones_done = C["sv"]

        brep_done = [0] * NCB
        crep_done = [0] * NCB
        row_mm = [0]

        def emit_bcast(sc, which, b_done, prod_done):
            dst = s_brep if which == 0 else s_crep
            rows = (0, 16) if which == 0 else (16, 32)

            for h in range(8):
                @blk.sync
                def _(q, sc=sc, rows=rows, h=h):
                    q.wait_ge(S["qbc"], bc_st)
                    q.wait_ge(S["sv"], ones_done)
                    if row_mm[0]:
                        q.wait_ge(S["st"], row_mm[0])
                    src = d_bc16[rows[0] + 2 * h:rows[0] + 2 * h + 2,
                                 sc * TB:(sc + 1) * TB] \
                        .rearrange("(o n) t -> o n t", o=1)
                    dma(q, "qbc", s_bcrow[:].rearrange("o (n t) -> o n t", n=2), src)
                row_ld = C["qbc"]

                mmj2 = [0, 0]
                for j2 in range(2):
                    j = 2 * h + j2
                    p = pC2 if j % 2 == 0 else pD

                    @blk.tensor
                    def _(t, sc=sc, j2=j2, p=p, w=row_ld):
                        t.wait_ge(S["qbc"], w)
                        t_guard(t, p)
                        i2 = t.matmul(p[:], s_ones[:],
                                      s_bcrow[:, j2 * TB:(j2 + 1) * TB],
                                      start=True, stop=True)
                        i3 = ms(i2, "st")
                        mmj2[j2] = i3
                        if j2 == 1:
                            row_mm[0] = i3

                @blk.scalar
                def _(s, sc=sc, h=h, mmj2=list(mmj2), dst=dst, which=which):
                    for j2 in range(2):
                        j = 2 * h + j2
                        p = pC2 if j % 2 == 0 else pD
                        s.wait_ge(S["st"], mmj2[j2])
                        if j == 0 and sc > 0:
                            if which == 0:
                                s.wait_ge(S["sv"], b_done[sc * 4 - 1])
                            else:
                                s.wait_ge(S["sv"], prod_done[sc * 4 - 1])
                        i2 = s.copy(dst[:, j * TB:(j + 1) * TB], p[:])
                        ms(i2, "ss")
                        set_free(p, "ss", C["ss"])
            if which == 0:
                brep_done[sc] = C["ss"]
            else:
                crep_done[sc] = C["ss"]

        exp_done = [0] * (NCB * 4)
        b_done = [0] * (NCB * 4)
        it_done = [0] * (NCB * 4)
        last_y = [None]
        du_done = [0] * NCB
        delta_done = [0] * NCB
        ld_xcsz = [0] * NCB
        ld_bc = [0] * NCB
        wdt_mm = [0] * NCB

        scan_done = [0] * (NCB * 4)
        prod_done = [0] * (NCB * 4)
        tree_done = [0] * (NCB * 4)

        def emit_dve_tail(itp):
            scp = itp // 4
            dtp = itp % 4
            parp = itp % 2
            v = _cur_v[0]
            v.wait_ge(S["sv"], prod_done[itp])
            if last_y[0] is not None:
                v.wait_ge(S[last_y[0][0]], last_y[0][1])

            def step(i2, v=v):
                ms(i2, "sv"); v.wait_ge(S["sv"], C["sv"])
            for half in (8, 4, 2):
                step(v.tensor_tensor(s_dA[parp][:, 0:half * TB],
                                     s_dA[parp][:, 0:half * TB],
                                     s_dA[parp][:, half * TB:2 * half * TB],
                                     ALU.add))
            step(v.tensor_tensor(s_ys[:], s_dA[parp][:, 0:TB],
                                 s_dA[parp][:, TB:2 * TB], ALU.add))
            step(v.scalar_tensor_tensor(s_ys[:],
                                        s_xcB[itp % 5][:],
                                        s_dd[:, dtp:dtp + 1], s_ys[:],
                                        ALU.mult, ALU.add))
            i2 = v.tensor_tensor(s_ybf[parp][:], s_ys[:],
                                 s_szB[itp % 5][:], ALU.mult)
            tree_done[itp] = it_done[itp] = ms(i2, "sv")

        def emit_ystore(itp):
            scp = itp // 4
            dtp = itp % 4
            parp = itp % 2

            @blk.sync
            def _(q, scp=scp, dtp=dtp, parp=parp, w=it_done[itp]):
                q.wait_ge(S["sv"], w)
                dma(q, f"cys{scp // 2}", d_yin[scp // 2][dtp * 128:(dtp + 1) * 128,
                                            (scp % 2) * TB:(scp % 2 + 1) * TB],
                    s_ybf[parp][:])
            last_y[0] = (f"cys{scp // 2}", C[f"cys{scp // 2}"])

        _cur_v = [None]
        emit_bcast(0, 0, b_done, prod_done)
        emit_bcast(0, 1, b_done, prod_done)
        for sc in range(NCB):
            @blk.sync
            def _(q, sc=sc):
                q.wait_ge(S["sv"], prol)
                for dt in range(4):
                    it = sc * 4 + dt
                    if it >= 5:
                        q.wait_ge(S["sv"], it_done[it - 5])
                    dma(q, "qbx", s_xcB[it % 5][:],
                        d_xc[dt * 128:(dt + 1) * 128, sc * TB:(sc + 1) * TB])
                    dma(q, "qbx", s_szB[it % 5][:],
                        d_sz[dt * 128:(dt + 1) * 128, sc * TB:(sc + 1) * TB])
            ld_xcsz[sc] = C["qbx"]

            @blk.sync
            def _(q, sc=sc):
                q.wait_ge(S["qbc"], bc_st)
                if sc >= 2 and wdt_mm[sc - 2]:
                    q.wait_ge(S["st"], wdt_mm[sc - 2])
                dma(q, f"qdt{sc % 2}", s_dtc[sc % 2][:], d_dtb[:, sc * TB:(sc + 1) * TB])
            dtc_ld = C[f"qdt{sc % 2}"]

            WDTP = [pA, pB, pC2, pD]
            mmd4 = [0] * 4
            for dt in range(4):
                p = WDTP[dt]

                @blk.tensor
                def _(t, sc=sc, dt=dt, p=p, dtc_ld=dtc_ld):
                    t.wait_ge(S[f"qdt{sc % 2}"], dtc_ld)
                    t_guard(t, p)
                    i2 = t.matmul(p[:], s_wdtT[:, dt * 128:(dt + 1) * 128],
                                  s_dtc[sc % 2][:],
                                  start=True, stop=True)
                    i3 = ms(i2, "st")
                    mmd4[dt] = i3
                    if dt == 3:
                        wdt_mm[sc] = i3

            @blk.scalar
            def _(s, sc=sc, mmd4=list(mmd4)):
                for dt in range(4):
                    s.wait_ge(S["st"], mmd4[dt])
                    if dt == 0 and sc > 0:
                        s.wait_ge(S["sv"], du_done[sc - 1])
                    ds = s_delta[:, dt * TB:(dt + 1) * TB]
                    i2 = s.activation(ds, WDTP[dt][:], AF.Exp, bias=s_bdt[:, dt:dt + 1])
                    ms(i2, "ss")
                    set_free(WDTP[dt], "ss", C["ss"])
                s.wait_ge(S["ss"], C["ss"])
                for dt in range(4):
                    ds = s_delta[:, dt * TB:(dt + 1) * TB]
                    i2 = s.activation(ds, ds, AF.Ln, bias=1.0)
                    ms(i2, "ss")
                s.wait_ge(S["ss"], C["ss"])
            delta_done[sc] = C["ss"]

            for dt in range(4):
                it = sc * 4 + dt
                par = it % 2

                @blk.scalar
                def _(s, sc=sc, dt=dt, it=it, par=par):
                    if it >= 2:
                        s.wait_ge(S["sv"], tree_done[it - 2])
                    else:
                        s.wait_ge(S["sv"], prol)
                    dv = s_dA[par][:].rearrange("p (n t) -> p n t", n=16)
                    i2 = None
                    for n in range(16):
                        i2 = s.activation(dv[:, n, :], s_delta[:, dt * TB:(dt + 1) * TB],
                                          AF.Exp, scale=s_aneg[:, dt * 16 + n:
                                                              dt * 16 + n + 1])
                    exp_done[it] = ms(i2, "ss")

                if it >= 1:
                    @blk.vector
                    def _(v, it=it, sc=sc):
                        v.wait_ge(S["ss"], crep_done[(it - 1) // 4])
                        v.wait_ge(S["sv"], scan_done[it - 1])
                        i2 = v.tensor_tensor(s_dA[(it - 1) % 2][:],
                                             s_dA[(it - 1) % 2][:], s_crep[:],
                                             ALU.mult)
                        prod_done[it - 1] = ms(i2, "sv")

                if dt == 0 and sc > 0:
                    emit_bcast(sc, 1, b_done, prod_done)

                if dt == 0:
                    if it >= 1:
                        @blk.vector
                        def _(v, it=it):
                            _cur_v[0] = v
                            emit_dve_tail(it - 1)
                        emit_ystore(it - 1)

                    @blk.vector
                    def _(v, sc=sc):
                        v.wait_ge(S["ss"], delta_done[sc])
                        v.wait_ge(S["qbx"], ld_xcsz[sc])
                        if sc > 0:
                            v.wait_ge(S["sv"], b_done[sc * 4 - 1])
                        i2 = None
                        for dt2 in range(4):
                            i2 = v.tensor_tensor(s_dU[:, dt2 * TB:(dt2 + 1) * TB],
                                                 s_delta[:, dt2 * TB:(dt2 + 1) * TB],
                                                 s_xcB[(sc * 4 + dt2) % 5][:],
                                                 ALU.mult)
                        ms(i2, "sv")
                    du_done[sc] = C["sv"]

                @blk.vector
                def _(v, sc=sc, dt=dt, it=it, par=par):
                    v.wait_ge(S["ss"], brep_done[sc])
                    v.wait_ge(S["sv"], du_done[sc])
                    dv = s_dA[par][:].rearrange("p (n t) -> p n t", n=16)

                    def step(i2, v=v):
                        ms(i2, "sv"); v.wait_ge(S["sv"], C["sv"])
                    dub = s_dU[:, dt * TB:(dt + 1) * TB] \
                        .rearrange("p (o t) -> p o t", o=1).broadcast_to([128, 16, TB])
                    step(v.tensor_tensor(s_bb[par][:].rearrange("p (n t) -> p n t", n=16),
                                         dub,
                                         s_brep[:].rearrange("p (n t) -> p n t", n=16),
                                         ALU.mult))
                    b_done[it] = C["sv"]
                    v.wait_ge(S["ss"], exp_done[it])
                    if sc > 0:
                        step(v.tensor_tensor(s_t16[:], dv[:, :, 0],
                                             s_car[:, dt * 16:(dt + 1) * 16], ALU.mult))
                        step(v.tensor_tensor(s_bb[par][:].rearrange(
                            "p (n t) -> p n t", n=16)[:, :, 0],
                            s_bb[par][:].rearrange("p (n t) -> p n t", n=16)[:, :, 0],
                            s_t16[:], ALU.add))
                        step(v.memset(dv[:, :, 0], 0.0))
                    step(v.tensor_tensor_scan(s_dA[par][:], s_dA[par][:], s_bb[par][:],
                                              0.0, ALU.mult, ALU.add))
                    i2 = v.tensor_copy(s_car[:, dt * 16:(dt + 1) * 16], dv[:, :, TB - 1])
                    scan_done[it] = ms(i2, "sv")

                if it >= 1 and dt > 0:
                    @blk.vector
                    def _(v, it=it):
                        _cur_v[0] = v
                        emit_dve_tail(it - 1)
                    emit_ystore(it - 1)

            if sc + 1 < NCB:
                emit_bcast(sc + 1, 0, b_done, prod_done)

            if sc >= 3 and sc % 2 == 1:
                b = (sc - 2) // 2

                @blk.gpsimd
                def _(g, b=b):
                    g.wait_ge(S[f"cys{b}"], 128)
                    g.collective_compute(
                        "AllGather", ALU.bypass, replica_groups=RG,
                        ins=[d_yin[b].ap().opt()],
                        outs=[d_ycat.ap()[b * DI:(b + 1) * DI, :].opt()]
                    ).then_inc(S[f"cy{b}"], 1)

        @blk.vector
        def _(v):
            v.wait_ge(S["sv"], scan_done[NCB * 4 - 1])
            i2 = v.tensor_tensor(s_dA[(NCB * 4 - 1) % 2][:],
                                 s_dA[(NCB * 4 - 1) % 2][:], s_crep[:], ALU.mult)
            prod_done[NCB * 4 - 1] = ms(i2, "sv")

        @blk.vector
        def _(v):
            _cur_v[0] = v
            emit_dve_tail(NCB * 4 - 1)
        emit_ystore(NCB * 4 - 1)
        for b in (6, 7):
            @blk.gpsimd
            def _(g, b=b):
                g.wait_ge(S[f"cys{b}"], 128)
                g.collective_compute(
                    "AllGather", ALU.bypass, replica_groups=RG,
                    ins=[d_yin[b].ap().opt()],
                    outs=[d_ycat.ap()[b * DI:(b + 1) * DI, :].opt()]
                ).then_inc(S[f"cy{b}"], 1)

        stB.close()
        emit_barrier()

        stC = ExitStack()
        sbC = lambda nm, sh, dt=F32: stC.enter_context(nc.sbuf_tensor(nm, sh, dt))
        s_g = sbC("s_g", [128, 4 * DM])
        s_s = sbC("s_s", [128, 4 * DM])
        s_borep = sbC("s_borep", [128, DM])

        @blk.sync
        def _(q):
            bo_b = d_bo.rearrange("(o x) -> o x", o=1).broadcast_to([128, DM])
            dma(q, "qbo", s_borep[:], bo_b)
        borep_ld = C["qbo"]

        rk_holder = {}

        @blk.gpsimd
        def _(g):
            g.wait_ge(S["cw3"], 1)
            g.wait_ge(S["cwo"], 1)
            for b in range(NCORE):
                g.wait_ge(S[f"ch{b}"], 1)
                g.wait_ge(S[f"cy{b}"], 1)
            rk_holder["rk"] = g.cc_rank(RG)

        outd = []
        last_cmb = [0]
        for cc in range(NCC):
            with ExitStack() as stw3:
                s_hidC = stw3.enter_context(nc.sbuf_tensor(f"s_hidC{cc}",
                                                           [128, NKH * TC], BF16))
                s_w3s = [stw3.enter_context(nc.sbuf_tensor(f"s_w3s{cc}_{i}",
                                                           [128, NKH * TC], BF16))
                         for i in range(2)]
                guard = {k: C[k] for k in ("st", "sv", "ss", "qo", "qcx0", "qcx1")}

                @blk.gpsimd
                def _(g, cc=cc, guard=guard):
                    for k, vv in guard.items():
                        if vv:
                            g.wait_ge(S[k], vv)
                    rk = rk_holder["rk"]
                    src = d_hcat.ap()[bass.ts(rk, HF), cc * TC:(cc + 1) * TC] \
                        .rearrange("(k p) t -> p k t", p=128)
                    dstv = s_hidC[:].rearrange("p (k t) -> p k t", k=NKH)
                    for i in range(4):
                        ks = (NKH * i) // 4
                        ke = (NKH * (i + 1)) // 4
                        dma(g, "qgh", dstv[:, ks:ke], src[:, ks:ke])
                hid_ld = C["qgh"]

                strip_free = [0, 0]
                for mb in range(4):
                    spar = mb % 2

                    @blk.gpsimd
                    def _(g, cc=cc, mb=mb, spar=spar):
                        if strip_free[spar]:
                            g.wait_ge(S["st"], strip_free[spar])
                        src = d_w3qt.ap()[:, mb * TC:(mb + 1) * TC] \
                            .rearrange("(k p) t -> p k t", p=128)
                        dstv = s_w3s[spar][:].rearrange("p (k t) -> p k t", k=NKH)
                        for i in range(4):
                            ks = (NKH * i) // 4
                            ke = (NKH * (i + 1)) // 4
                            dma(g, f"qgs{spar}", dstv[:, ks:ke], src[:, ks:ke])
                    strip_ld = C[f"qgs{spar}"]

                    for tt in range(4):
                        p = pA if (mb * 4 + tt) % 2 == 0 else pB

                        @blk.tensor
                        def _(t, cc=cc, mb=mb, tt=tt, p=p, spar=spar,
                              strip_ld=strip_ld, hid_ld=hid_ld):
                            t.wait_ge(S[f"qgs{spar}"], strip_ld)
                            t.wait_ge(S["qgh"], hid_ld)
                            t_guard(t, p)
                            for kh in range(NKH):
                                i2 = t.matmul(p[:],
                                              s_hidC[:, kh * TC + tt * 128:
                                                     kh * TC + (tt + 1) * 128],
                                              s_w3s[spar][:, kh * TC:kh * TC + 512],
                                              start=(kh == 0), stop=(kh == NKH - 1))
                                if kh == NKH - 1:
                                    ms(i2, "st")
                        mmd = C["st"]
                        strip_free[spar] = mmd

                        @blk.scalar
                        def _(s, mb=mb, tt=tt, p=p, mmd=mmd):
                            s.wait_ge(S["st"], mmd)
                            if last_cmb[0]:
                                s.wait_ge(S["sv"], last_cmb[0])
                            i2 = s.activation(s_g[:, tt * DM + mb * 512:
                                                  tt * DM + (mb + 1) * 512],
                                              p[:], AF.Silu)
                            ms(i2, "ss")
                        set_free(p, "ss", C["ss"])

            with ExitStack() as stwo:
                s_yc = stwo.enter_context(nc.sbuf_tensor(f"s_yc{cc}",
                                                         [128, NKY * TC], BF16))
                s_wos = [stwo.enter_context(nc.sbuf_tensor(f"s_wos{cc}_{i}",
                                                           [128, NKY * TC], BF16))
                         for i in range(2)]
                guard = {k: C[k] for k in ("st", "sv", "ss", "qo", "qcx0", "qcx1")}

                @blk.gpsimd
                def _(g, cc=cc, guard=guard):
                    for k, vv in guard.items():
                        if vv:
                            g.wait_ge(S[k], vv)
                    rk = rk_holder["rk"]
                    src = d_ycat.ap()[bass.ts(rk, DI), cc * TC:(cc + 1) * TC] \
                        .rearrange("(k p) t -> p k t", p=128)
                    dstv = s_yc[:].rearrange("p (k t) -> p k t", k=NKY)
                    for i in range(4):
                        ks = (NKY * i) // 4
                        ke = (NKY * (i + 1)) // 4
                        dma(g, "qgy", dstv[:, ks:ke], src[:, ks:ke])
                y_ld = C["qgy"]

                strip_free = [0, 0]
                for mb in range(4):
                    spar = mb % 2

                    @blk.gpsimd
                    def _(g, cc=cc, mb=mb, spar=spar):
                        if strip_free[spar]:
                            g.wait_ge(S["st"], strip_free[spar])
                        src = d_wot.ap()[:, mb * TC:(mb + 1) * TC] \
                            .rearrange("(k p) t -> p k t", p=128)
                        dstv = s_wos[spar][:].rearrange("p (k t) -> p k t", k=NKY)
                        for i in range(4):
                            ks = (NKY * i) // 4
                            ke = (NKY * (i + 1)) // 4
                            dma(g, f"qgw{spar}", dstv[:, ks:ke], src[:, ks:ke])
                    strip_ld = C[f"qgw{spar}"]

                    for tt in range(4):
                        p = pA if (mb * 4 + tt) % 2 == 0 else pB

                        @blk.tensor
                        def _(t, mb=mb, tt=tt, p=p, spar=spar,
                              strip_ld=strip_ld, y_ld=y_ld):
                            t.wait_ge(S[f"qgw{spar}"], strip_ld)
                            t.wait_ge(S["qgy"], y_ld)
                            t_guard(t, p)
                            for ky in range(NKY):
                                i2 = t.matmul(p[:],
                                              s_yc[:, ky * TC + tt * 128:
                                                   ky * TC + (tt + 1) * 128],
                                              s_wos[spar][:, ky * TC:ky * TC + 512],
                                              start=(ky == 0), stop=(ky == NKY - 1))
                                if ky == NKY - 1:
                                    ms(i2, "st")
                        mmd = C["st"]
                        strip_free[spar] = mmd

                        @blk.vector
                        def _(v, mb=mb, tt=tt, p=p, mmd=mmd):
                            v.wait_ge(S["st"], mmd)
                            v.wait_ge(S["qbo"], borep_ld)
                            i2 = v.tensor_tensor(s_s[:, tt * DM + mb * 512:
                                                     tt * DM + (mb + 1) * 512],
                                                 p[:], s_borep[:, mb * 512:
                                                               (mb + 1) * 512],
                                                 ALU.add)
                            ms(i2, "sv")
                        set_free(p, "sv", C["sv"])
            wo_pass_done_sv = C["sv"]
            g_pass_done_ss = C["ss"]

            with ExitStack() as stcm:
                s_xq = [stcm.enter_context(nc.sbuf_tensor(f"s_xq{cc}_{i}",
                                                          [128, DM], F32)) for i in range(2)]
                s_res = [stcm.enter_context(nc.sbuf_tensor(f"s_res{cc}_{i}",
                                                           [128, DM], F32)) for i in range(2)]
                for tt in range(4):
                    @blk.sync
                    def _(q, cc=cc, tt=tt, wo=wo_pass_done_sv):
                        q.wait_ge(S["sv"], wo)
                        if outd:
                            q.wait_ge(S["qo"], outd[-1])
                        dma(q, f"qcx{tt % 2}", s_xq[tt % 2][:],
                            d_xmy[cc * TC + tt * 128:cc * TC + (tt + 1) * 128, :])
                    xql = C[f"qcx{tt % 2}"]

                    @blk.vector
                    def _(v, tt=tt, xql=xql, wo=wo_pass_done_sv, gp=g_pass_done_ss):
                        v.wait_ge(S[f"qcx{tt % 2}"], xql)
                        v.wait_ge(S["ss"], gp)
                        i2 = v.tensor_tensor(s_res[tt % 2][:], s_g[:, tt * DM:(tt + 1) * DM],
                                                 s_s[:, tt * DM:(tt + 1) * DM], ALU.mult)
                        ms(i2, "sv"); v.wait_ge(S["sv"], C["sv"])
                        i2 = v.tensor_tensor(s_res[tt % 2][:], s_res[tt % 2][:],
                                             s_xq[tt % 2][:], ALU.add)
                        ms(i2, "sv")
                    cmb = C["sv"]

                    @blk.sync
                    def _(q, cc=cc, tt=tt, cmb=cmb):
                        q.wait_ge(S["sv"], cmb)
                        dma(q, "qo", d_out[cc * TC + tt * 128:cc * TC + (tt + 1) * 128, :],
                            s_res[tt % 2][:])
                    outd.append(C["qo"])
                    last_cmb[0] = cmb

        stC.close()

        @blk.sync
        def _(q):
            q.wait_ge(S["qo"], C["qo"])

    return nc


def _host_slices(inputs, r):
    f32 = np.float32
    x = np.ascontiguousarray(np.asarray(inputs["L0_attn_hidden_states"], f32)[0])
    W_in = np.asarray(inputs["L0_attn_W_in"], f32)
    conv_w = np.asarray(inputs["L0_attn_conv_w"], f32)
    conv_b = np.asarray(inputs["L0_attn_conv_b"], f32)
    W_x = np.asarray(inputs["L0_attn_W_x"], f32)
    W_dt = np.asarray(inputs["L0_attn_W_dt"], f32)
    b_dt = np.asarray(inputs["L0_attn_b_dt"], f32)
    A_log = np.asarray(inputs["L0_attn_A_log"], f32)
    Dp = np.asarray(inputs["L0_attn_D"], f32)
    W_out = np.asarray(inputs["L0_attn_W_out"], f32)
    b_out = np.asarray(inputs["L0_attn_b_out"], f32)
    w1 = np.asarray(inputs["L0_ffn_w1"], f32)
    w2 = np.asarray(inputs["L0_ffn_w2"], f32)
    w3 = np.asarray(inputs["L0_ffn_w3"], f32)

    ds = slice(r * DSL, (r + 1) * DSL)
    hs = slice(r * HSL, (r + 1) * HSL)
    per_dt = lambda a: np.ascontiguousarray(
        a[ds].reshape(4, 128, -1).transpose(1, 0, 2).reshape(128, -1))
    m = {
        "x_full": x,
        "x_myslice": np.ascontiguousarray(x[r * TSL:(r + 1) * TSL]),
        "w_in_sl": np.ascontiguousarray(np.concatenate([W_in[ds], W_in[DI:][ds]], 0)),
        "conv_w_t": per_dt(conv_w),
        "conv_b_t": per_dt(conv_b[:, None]),
        "w_x_sl": np.ascontiguousarray(W_x[:, ds]),
        "w_dt_sl": np.ascontiguousarray(W_dt[ds]),
        "b_dt_t": per_dt(b_dt[:, None]),
        "a_log_t": per_dt(A_log),
        "d_t": per_dt(Dp[:, None]),
        "w_out_sl": np.ascontiguousarray(W_out[:, ds]),
        "b_out": np.ascontiguousarray(b_out),
        "w1_sl": np.ascontiguousarray(w1[hs]),
        "w2_sl": np.ascontiguousarray(w2[hs]),
        "w3_sl": np.ascontiguousarray(w3[:, hs]),
        "ident": np.eye(128, dtype=BFD),
        "consts": np.tile(np.array([[MAGIC, 1.0, EPS_LN, 0.0]], np.float32), (128, 1)),
    }
    return m


def kernel(**inputs):
    nc = _build()
    in_maps = [_host_slices(inputs, r) for r in range(NCORE)]
    res = run_bass_kernel_spmd(nc, in_maps, core_ids=list(range(NCORE)))
    out = np.concatenate([np.asarray(res.results[r]["out_sl"]) for r in range(NCORE)],
                         axis=0)
    return out.reshape(1, L, DM).astype(np.float32, copy=False)
